# version 02
# baseline (speedup 1.0000x reference)
import numpy as np
import ml_dtypes
import concourse.bass as bass
import concourse.mybir as mybir

F32 = mybir.dt.float32
BF16 = mybir.dt.bfloat16
I32 = mybir.dt.int32
AF = mybir.ActivationFunctionType
ALU = mybir.AluOpType
AX = mybir.AxisListType

D = 2048
NCC = D // 128
SOWN = 2048
SALL = 4096
NSUB_KV = SALL // 512
NQC = SOWN // 512
I_RAW = 6464
I_PAD = 6528
NOT = I_PAD // 128
TC = 512
NCHUNK = SOWN // TC
W_COUNT = float(I_RAW * D)
EPS = 1e-5

ENGS = ("sync", "scalar", "vector", "tensor", "gpsimd")


class Sem:
    def __init__(self, nc, name):
        self._g = nc.semaphore(name)
        self.h = self._g.__enter__()
        self.n = 0

    def next(self, k=1):
        self.n += k
        return self.n


class Prog:
    def __init__(self, nc, sems):
        self.nc = nc
        self.q = {e: [] for e in ENGS}
        self.sems = sems

    def on(self, eng, fn):
        self.q[eng].append(fn)

    def op(self, eng, fn, sem=None, k=1):
        if sem is None:
            self.on(eng, fn)
            return None
        m = sem.next(k)
        self.on(eng, lambda E, fn=fn, sem=sem, k=k: fn(E).then_inc(sem.h, k))
        return m

    def opw(self, eng, fn, sem, k=1):
        m = self.op(eng, fn, sem, k)
        self.wait(eng, sem, m)
        return m

    def vop(self, fn, sem):
        m = self.op("vector", fn, sem)
        self.wait("vector", sem, m)
        return m

    def wait(self, eng, sem, milestone):
        if milestone is None or milestone <= 0:
            return
        assert milestone <= sem.n, f"future milestone {milestone} > {sem.n}"
        self.on(eng, lambda E, s=sem, m=milestone: E.wait_ge(s.h, m))

    def snap(self):
        return {s: s.n for s in self.sems}

    def wait_snap(self, eng, snap):
        for s, n in snap.items():
            if n > 0:
                self.wait(eng, s, n)

    def emit(self, blk):
        for e in ENGS:
            fns = self.q[e]
            def body(E, fns=fns):
                for f in fns:
                    f(E)
            getattr(blk, e)(body)


class Buf:
    def __init__(self, tile, name=""):
        self.tile = tile
        self.name = name
        self.w = None
        self.r = None

    def wait_w(self, p, eng):
        if self.w is not None:
            p.wait(eng, self.w[0], self.w[1])

    def wait_r(self, p, eng):
        if self.r is not None:
            p.wait(eng, self.r[0], self.r[1])


class Arena:
    def __init__(self, nc, p, total_f32):
        self._g = nc.sbuf_tensor("arena", [128, total_f32], F32)
        self.base = self._g.__enter__()
        self.base_bf = self.base.tensor.bitcast(BF16) if hasattr(self.base, "tensor") else None
        self.nc = nc
        self.p = p
        self.total = total_f32
        self.free_list = [[0, total_f32, None]]
        self.regions = {}

    def _view(self, st, n, dtype):
        if dtype == F32:
            return self.base[:, st:st + n]
        if dtype == BF16:
            return self.base.bitcast(BF16)[:, 2 * st:2 * (st + n)]
        if dtype == I32:
            return self.base.bitcast(I32)[:, st:st + n]
        raise ValueError(dtype)

    def alloc(self, name, n_f32, dtype=F32, align=4):
        n_f32 = (n_f32 + 3) & ~3
        for i, seg in enumerate(self.free_list):
            st, ln, fence = seg
            pad = (-st) % align
            if ln >= n_f32 + pad:
                if pad:
                    self.free_list[i] = [st + pad, ln - pad, fence]
                    st, ln = st + pad, ln - pad
                    seg = self.free_list[i]
                seg[0] = st + n_f32
                seg[1] = ln - n_f32
                if seg[1] == 0:
                    self.free_list.pop(i)
                self.regions[name] = (st, n_f32)
                return self._view(st, n_f32, dtype), fence
        raise RuntimeError(f"arena OOM: {name} needs {n_f32 * 4}B; free={[(s[0], s[1]) for s in self.free_list]}")

    @staticmethod
    def _merge_fence(a, b):
        if a is None:
            return dict(b) if b else None
        if b is None:
            return dict(a)
        out = dict(a)
        for s, n in b.items():
            out[s] = max(out.get(s, 0), n)
        return out

    def free(self, name):
        st, ln = self.regions.pop(name)
        self.free_list.append([st, ln, self.p.snap()])
        self.free_list.sort(key=lambda s: s[0])
        merged = []
        for seg in self.free_list:
            if merged and merged[-1][0] + merged[-1][1] == seg[0]:
                merged[-1][1] += seg[1]
                merged[-1][2] = self._merge_fence(merged[-1][2], seg[2])
            else:
                merged.append(seg)
        self.free_list = merged


def build(debug=0):
    nc = bass.Bass("TRN2", target_bir_lowering=False, detect_race_conditions=False)

    XT_OWN = nc.dram_tensor("xt_own", [D, SOWN], F32, kind="ExternalInput")
    XT_NB = nc.dram_tensor("xt_nb", [D, SOWN], F32, kind="ExternalInput")
    WK = nc.dram_tensor("wk", [NCC, 128, 128], F32, kind="ExternalInput")
    WV = nc.dram_tensor("wv", [NCC, 128, 128], F32, kind="ExternalInput")
    BK = nc.dram_tensor("bk", [128, 1], F32, kind="ExternalInput")
    BV = nc.dram_tensor("bv", [128, 1], F32, kind="ExternalInput")
    WO = nc.dram_tensor("wo", [16, 128, D], F32, kind="ExternalInput")
    BO_T = nc.dram_tensor("bo_t", [128, 16], F32, kind="ExternalInput")
    WGU_T = nc.dram_tensor("wgu_t", [NOT, 2, 128, D], F32, kind="ExternalInput")
    WD_T = nc.dram_tensor("wd_t", [16, 128, I_PAD], F32, kind="ExternalInput")
    OUT_T = nc.dram_tensor("out_t", [D, SOWN], F32, kind="ExternalOutput")

    X1_HBM = nc.dram_tensor("x1_scratch", [D, SOWN], F32, kind="Internal")
    TOK_HBM = nc.dram_tensor("tok_scratch", [128, 16], F32, kind="Internal")
    TGU_HBM = nc.dram_tensor("tgu_c", [NOT, 2, 128, D], BF16, kind="Internal")
    TD_HBM = nc.dram_tensor("td_c", [16, 128, I_PAD], BF16, kind="Internal")

    dbg = {}
    if debug:
        for nm, shp in (("k", [128, SALL]), ("u", [128, SALL]), ("x1", [D, SOWN]),
                        ("xq", [128, NCC * 512]), ("hq", [128, NOT * 512]), ("scl", [128, 16]),
                        ("z", [4, 512]), ("rz", [4, 512]), ("expt", [128, 32 * 512]),
                        ("obuf", [4, 128, 512]), ("h", [128, NOT * 512]), ("tg0", [128, D]),
                        ("wgu0", [128, 2 * D]), ("wgu0b", [128, 2 * D]), ("wgu0c", [128, 2 * D]), ("tp0", [128, D]), ("tq0", [128, D]),
                        ("tokv", [128, 16]), ("habs", [1, 512]), ("mxh", [128, 512]),
                        ("pt0", [128, 128]),
                        ("tu0", [128, D]), ("cbb", [128, 512]), ("c3b", [128, 512])):
            dbg[nm] = nc.dram_tensor("dbg_" + nm, shp, F32, kind="ExternalOutput")

    IDENT = nc.inline_tensor(np.eye(128, dtype=ml_dtypes.bfloat16), "c_ident")
    IDENT32 = nc.inline_tensor(np.eye(128, dtype=np.float32), "c_ident32")
    ONESC32 = nc.inline_tensor(np.ones((128, 1), dtype=np.float32), "c_onesc32")
    ONESCB = nc.inline_tensor(np.ones((128, 1), dtype=ml_dtypes.bfloat16), "c_onescb")
    ONESR32 = nc.inline_tensor(np.ones((1, 128), dtype=np.float32), "c_onesr32")

    s_ldw = Sem(nc, "s_ldw")
    s_ldx = Sem(nc, "s_ldx")
    s_gp = Sem(nc, "s_gp")
    s_pe = Sem(nc, "s_pe")
    s_dve = Sem(nc, "s_dve")
    s_act = Sem(nc, "s_act")
    SEMS = [s_ldw, s_ldx, s_gp, s_pe, s_dve, s_act]

    p = Prog(nc, SEMS)
    guards = []

    def psum(name, shape, dtype=F32):
        g = nc.psum_tensor(name, shape, dtype)
        t = g.__enter__()
        guards.append(g)
        return Buf(t[:], name)

    ar = Arena(nc, p, 52000)

    def AL(name, n, dtype=F32, eng_fence=(), align=4):
        view, fence = ar.alloc(name, n, dtype, align)
        if fence is not None:
            for e in eng_fence:
                p.wait_snap(e, fence)
        return view

    ident = AL("ident", 64, BF16)
    ident32 = AL("ident32", 128)
    onesc32 = AL("onesc32", 4)[:, 0:1]
    onescb = AL("onescb", 4, BF16)[:, 0:2][:, 0:1]
    onesr32 = AL("onesr32", 128)[0:1, 0:128]
    bk_c = AL("bk_c", 4)[:, 0:1]
    bv_c = AL("bv_c", 4)[:, 0:1]
    bo_t = AL("bo_t", 16)
    ncsk_c = AL("ncsk_c", 4)[:, 0:1]
    ncsv_c = AL("ncsv_c", 4)[:, 0:1]
    scl = AL("scl", 16)
    stripA = AL("stripA", 512)
    stripB = AL("stripB", 512)
    stripC = AL("stripC", 512)
    stripD = AL("stripD", 512)
    tokv = AL("tokv", 16)


    wk_b = AL("wk_b", D // 2, BF16)
    wv_b = AL("wv_b", D // 2, BF16)
    wo_b = AL("wo_b", D // 2, BF16)
    kT = AL("kT", SALL // 2, BF16)
    uT = AL("uT", SALL // 2, BF16)
    uN = AL("uN", SALL // 2, BF16)

    PA = psum("psA", [128, 512]); PB = psum("psB", [128, 512])
    PC = psum("psC", [128, 512]); PD = psum("psD", [128, 512])
    PE_ = psum("psE", [128, 512])
    PS1 = psum("psS1", [1, 512]); PS2 = psum("psS2", [1, 512])
    PT = psum("psT", [128, 128], BF16)

    def pe_fence(cell):
        m = p.op("tensor", lambda E, cell=cell: E.matmul(cell, onesr32[0:1, 0:1],
                                                         onesr32[0:1, 0:1], start=True, stop=True), s_pe)
        return m

    def ones_colsum(acc_buf, PSx, dst_strip, extra_dep=None):
        PSx.wait_r(p, "tensor")
        acc_buf.wait_w(p, "tensor")
        if extra_dep:
            p.wait("tensor", extra_dep[0], extra_dep[1])
        mmx = p.op("tensor", lambda E: E.matmul(PSx.tile[0:1, 0:512], onesc32, acc_buf.tile[:],
                                                start=True, stop=True), s_pe)
        PSx.w = (s_pe, mmx)
        acc_buf.r = (s_pe, mmx)
        PSx.wait_w(p, "vector")
        mev = p.opw("vector", lambda E: E.tensor_copy(dst_strip[0:1, 0:512], PSx.tile[0:1, 0:512]), s_dve)
        PSx.r = (s_dve, mev)
        return mev

    def bcast(strip_src, dst_tile, PSx, dep_sem, dep_m, n=512):
        PSx.wait_r(p, "tensor")
        p.wait("tensor", dep_sem, dep_m)
        mmx = p.op("tensor", lambda E: E.matmul(PSx.tile[:, 0:n], onesr32, strip_src[0:1, 0:n],
                                                start=True, stop=True), s_pe)
        PSx.w = (s_pe, mmx)
        PSx.wait_w(p, "vector")
        mev = p.opw("vector", lambda E: E.tensor_copy(dst_tile[:, 0:n], PSx.tile[:, 0:n]), s_dve)
        PSx.r = (s_dve, mev)
        return mev

    tokbuf = Buf(None, "tokbuf")

    def bounce_strips(col0, k, dst_strip_ap, dep_sem, dep_m):
        tokbuf.wait_r(p, "gpsimd")
        tokbuf.wait_w(p, "gpsimd")
        p.wait("gpsimd", dep_sem, dep_m)
        mo = p.op("gpsimd", lambda E: E.dma_start(
            bass.AP(TOK_HBM, 0, [[16, 128], [1, k]]), tokv[:, col0:col0 + k]), s_gp, 16)
        tokbuf.w = (s_gp, mo)
        p.wait("gpsimd", s_gp, mo)
        mb = p.op("gpsimd", lambda E: E.dma_start(
            dst_strip_ap, bass.AP(TOK_HBM, 0, [[1, 1], [1, k], [16, 128]]),
            allow_slow_non_contiguous=True), s_gp, 16)
        tokbuf.r = (s_gp, mb)
        return (s_gp, mb)

    for (dst, src) in ((ident, IDENT), (ident32, IDENT32), (onesc32, ONESC32),
                       (onescb, ONESCB), (bk_c, BK), (bv_c, BV), (bo_t, BO_T)):
        p.op("sync", lambda E, dst=dst, src=src: E.dma_start(dst, src.ap()), s_ldw, 16)
    p.op("sync", lambda E: E.dma_start(onesr32, ONESR32.ap()), s_ldw, 16)

    wkv32 = Buf(AL("wkv32", 2 * D), "wkv32")
    p.op("sync", lambda E: E.dma_start(
        wkv32.tile[:, 0:D].rearrange("p (cc d) -> p cc d", cc=NCC), WK.ap().transpose([1, 0, 2])), s_ldw, 16)
    p.op("sync", lambda E: E.dma_start(
        wkv32.tile[:, D:2 * D].rearrange("p (cc d) -> p cc d", cc=NCC), WV.ap().transpose([1, 0, 2])), s_ldw, 16)
    p.wait("scalar", s_ldw, s_ldw.n)
    p.op("scalar", lambda E: E.copy(wk_b[:], wkv32.tile[:, 0:D]), s_act)
    m_wkv_cast = p.opw("scalar", lambda E: E.copy(wv_b[:], wkv32.tile[:, D:2 * D]), s_act)

    def build_cs(src_b, out_col):
        PS1.wait_r(p, "tensor")
        p.wait("tensor", s_act, m_wkv_cast)
        def mm(E, src_b=src_b):
            for cc in range(NCC):
                i = E.matmul(PS1.tile[0:1, 0:128], onescb, src_b[:, cc * 128:(cc + 1) * 128],
                             start=(cc == 0), stop=(cc == NCC - 1))
            return i
        mm1 = p.op("tensor", mm, s_pe)
        PS1.w = (s_pe, mm1)
        PS1.wait_w(p, "vector")
        mneg = p.op("vector", lambda E: E.tensor_scalar(stripA[0:1, 0:128], PS1.tile[0:1, 0:128],
                                                        -1.0, None, ALU.mult), s_dve)
        PS1.r = (s_dve, mneg)
        PE_.wait_r(p, "tensor")
        p.wait("tensor", s_dve, mneg)
        p.op("tensor", lambda E: E.transpose(PE_.tile[:, 0:1], stripA[0:1, 0:128], ident32[0:1, 0:1]), s_pe)
        mtr = pe_fence(PE_.tile[0:1, 511:512])
        PE_.w = (s_pe, mtr)
        PE_.wait_w(p, "vector")
        mc = p.op("vector", lambda E, out_col=out_col: E.tensor_copy(out_col, PE_.tile[:, 0:1]), s_dve)
        PE_.r = (s_dve, mc)
        return mc

    build_cs(wk_b, ncsk_c)
    m_cs_done = build_cs(wv_b, ncsv_c)

    wo_st = [Buf(AL(f"wo_st{i}", D), f"wo_st{i}") for i in range(2)]
    wo_acc = Buf(AL("wo_acc", D), "wo_acc")
    mr = None
    for h in range(16):
        Wb = wo_st[h % 2]
        Wb.wait_r(p, "sync")
        mw = p.op("sync", lambda E, h=h, Wb=Wb: E.dma_start(Wb.tile[:], WO.ap()[h]), s_ldw, 16)
        Wb.w = (s_ldw, mw)
        Wb.wait_w(p, "vector")
        if h == 0:
            mr = p.op("vector", lambda E, Wb=Wb: E.tensor_copy(wo_acc.tile[:], Wb.tile[:]), s_dve)
        else:
            mr = p.op("vector", lambda E, Wb=Wb: E.tensor_add(wo_acc.tile[:], wo_acc.tile[:], Wb.tile[:]), s_dve)
        Wb.r = (s_dve, mr)
    p.wait("scalar", s_dve, mr)
    m_wo = p.opw("scalar", lambda E: E.copy(wo_b[:], wo_acc.tile[:]), s_act)
    ar.free("wkv32"); ar.free("wo_st0"); ar.free("wo_st1"); ar.free("wo_acc")

    CH_F = 3264
    scan_st = [Buf(AL(f"scan{i}", CH_F, eng_fence=("sync",)), f"scan{i}") for i in range(2)]
    scan_acc = AL("scan_acc", 136)
    scan_m = AL("scan_m", 4)
    ma = None
    scan_plan = []
    for ot in range(NOT):
        scan_plan.append((WGU_T.ap()[ot, 0], D, ot))
        scan_plan.append((WGU_T.ap()[ot, 1], D, 51 + ot))
    for nt in range(16):
        for hh in range(2):
            scan_plan.append((WD_T.ap()[nt][:, hh * 3264:(hh + 1) * 3264], 3264, 102 + 2 * nt + hh))
    for i, (apv, width, col) in enumerate(scan_plan):
        Sb_ = scan_st[i % 2]
        Sb_.wait_r(p, "sync")
        mw = p.op("sync", lambda E, Sb_=Sb_, apv=apv, width=width: E.dma_start(
            Sb_.tile[:, 0:width], apv), s_ldw, 16)
        Sb_.w = (s_ldw, mw)
        Sb_.wait_w(p, "scalar")
        ma = p.op("scalar", lambda E, Sb_=Sb_, col=col, width=width: E.activation(
            Sb_.tile[:, 0:width], Sb_.tile[:, 0:width], AF.Abs, scale=1.0,
            accum_out=scan_acc[:, col:col + 1]), s_act)
        Sb_.r = (s_act, ma)
    m_scan_done = ma
    p.wait("vector", s_act, m_scan_done)
    mrd = None
    for mi, (lo, hi) in enumerate(((0, 51), (51, 102), (102, 134))):
        mrd = p.op("vector", lambda E, mi=mi, lo=lo, hi=hi: E.tensor_reduce(
            scan_m[:, mi:mi + 1], scan_acc[:, lo:hi], AX.X, ALU.add), s_dve)
    PS2.wait_r(p, "tensor")
    p.wait("tensor", s_dve, mrd)
    mms = p.op("tensor", lambda E: E.matmul(PS2.tile[0:1, 0:3], onesc32, scan_m[:, 0:3],
                                            start=True, stop=True), s_pe)
    PS2.w = (s_pe, mms)
    PS2.wait_w(p, "vector")
    mcm = p.op("vector", lambda E: E.tensor_scalar(stripD[0:1, 8:11], PS2.tile[0:1, 0:3],
                                                   1.0 / W_COUNT, EPS, ALU.mult, ALU.max), s_dve)
    PS2.r = (s_dve, mcm)
    PD.wait_r(p, "tensor")
    p.wait("tensor", s_dve, mcm)
    mbc2 = p.op("tensor", lambda E: E.matmul(PD.tile[:, 0:3], onesr32, stripD[0:1, 8:11],
                                             start=True, stop=True), s_pe)
    PD.w = (s_pe, mbc2)
    PD.wait_w(p, "vector")
    p.vop(lambda E: E.tensor_copy(scl[:, 6:7], PD.tile[:, 0:1]), s_dve)
    p.vop(lambda E: E.tensor_scalar(scl[:, 7:8], PD.tile[:, 1:2], 127.0, None, ALU.mult), s_dve)
    p.vop(lambda E: E.tensor_scalar(scl[:, 8:9], PD.tile[:, 2:3], 1.0 / 127.0, None, ALU.mult), s_dve)
    p.vop(lambda E: E.tensor_copy(scl[:, 9:10], PD.tile[:, 1:2]), s_dve)
    mg3 = p.op("vector", lambda E: E.reciprocal(scl[:, 0:3], PD.tile[:, 0:3]), s_dve)
    PD.r = (s_dve, mg3)
    p.wait("vector", s_dve, mg3)
    m_scales = p.op("vector", lambda E: E.tensor_scalar(scl[:, 3:6], scl[:, 0:3], -1.0, None, ALU.mult), s_dve)
    if debug:
        p.wait("gpsimd", s_dve, m_scales)
        p.op("gpsimd", lambda E: E.dma_start(dbg["scl"].ap(), scl[:]), s_gp, 16)

    xst = [Buf(AL(f"xst{i}", NCC * 512), f"xst{i}") for i in range(2)]
    xb_b = Buf(AL("xb", NCC * 512 // 2, BF16), "xb")
    sq_b = Buf(AL("sq", 2048), "sq")
    tr1 = Buf(AL("tr1", 512), "tr1")
    tr2 = Buf(AL("tr2", 512), "tr2")
    rb_b = AL("rb_b", 512)
    mb_b = AL("mb_b", 512)
    ev1 = AL("ev1", 512)
    ev2 = AL("ev2", 512)

    def xsub_ap(src, sub):
        return bass.AP(src, sub * 512, [[SOWN, 128], [128 * SOWN, NCC], [1, 512]])

    m_kT_w = m_uT_w = m_uN_w = None
    for sub in range(NSUB_KV):
        src = XT_OWN if sub < 4 else XT_NB
        lsub = sub % 4
        X_b = xst[sub % 2]
        X_b.wait_r(p, "scalar")
        mx = p.op("scalar", lambda E, X_b=X_b, src=src, lsub=lsub: E.dma_start(
            X_b.tile.rearrange("p (cc t) -> p cc t", cc=NCC), xsub_ap(src, lsub)), s_ldx, 16)
        X_b.w = (s_ldx, mx)

        X_b.wait_w(p, "scalar")
        xb_b.wait_r(p, "scalar")
        mcast = p.opw("scalar", lambda E, X_b=X_b: E.copy(xb_b.tile[:], X_b.tile[:]), s_act)
        xb_b.w = (s_act, mcast)

        X_b.wait_w(p, "vector")
        tr1.wait_r(p, "vector")
        def tree_sum(E, X_b=X_b):
            E.tensor_add(tr1.tile[:], X_b.tile[:, 0:512], X_b.tile[:, 512:1024])
            for cc in range(2, NCC):
                i = E.tensor_add(tr1.tile[:], tr1.tile[:], X_b.tile[:, cc * 512:(cc + 1) * 512])
            return i
        mt1 = p.op("vector", tree_sum, s_dve)
        tr1.w = (s_dve, mt1)
        tr2.wait_r(p, "vector")
        mt2 = None
        for qtr in range(4):
            sq_b.wait_r(p, "scalar")
            X_b.wait_w(p, "scalar")
            msq = p.op("scalar", lambda E, X_b=X_b, qtr=qtr: E.square(
                sq_b.tile[:], X_b.tile[:, qtr * 2048:(qtr + 1) * 2048]), s_act)
            sq_b.w = (s_act, msq)
            p.wait("vector", s_act, msq)
            def tree_sq(E, qtr=qtr):
                if qtr == 0:
                    E.tensor_add(tr2.tile[:], sq_b.tile[:, 0:512], sq_b.tile[:, 512:1024])
                    rng_ = range(2, 4)
                else:
                    rng_ = range(0, 4)
                for cc in rng_:
                    i = E.tensor_add(tr2.tile[:], tr2.tile[:], sq_b.tile[:, cc * 512:(cc + 1) * 512])
                return i
            mt2 = p.op("vector", tree_sq, s_dve)
            sq_b.r = (s_dve, mt2)
        tr2.w = (s_dve, mt2)
        X_b.r = (s_dve, mt2)

        ones_colsum(tr1, PS1, stripA)
        mss = ones_colsum(tr2, PS2, stripB)
        p.vop(lambda E: E.tensor_scalar(stripA[0:1, 0:512], stripA[0:1, 0:512], 1.0 / D, None, ALU.mult), s_dve)
        p.vop(lambda E: E.tensor_scalar(stripB[0:1, 0:512], stripB[0:1, 0:512], 1.0 / D, EPS, ALU.mult, ALU.add), s_dve)
        p.vop(lambda E: E.tensor_mul(stripC[0:1, 0:512], stripA[0:1, 0:512], stripA[0:1, 0:512]), s_dve)
        mvar = p.vop(lambda E: E.tensor_sub(stripB[0:1, 0:512], stripB[0:1, 0:512], stripC[0:1, 0:512]), s_dve)
        p.wait("scalar", s_dve, mvar)
        mla = p.op("scalar", lambda E: E.activation(stripB[0:1, 0:512], stripB[0:1, 0:512], AF.Ln), s_act)
        p.wait("scalar", s_act, mla)
        mr_ = p.op("scalar", lambda E: E.activation(stripB[0:1, 0:512], stripB[0:1, 0:512], AF.Exp, scale=-0.5), s_act)
        bcast(stripA, mb_b, PC, s_dve, mvar)
        m_bc = bcast(stripB, rb_b, PD, s_act, mr_)

        for wi, (wsrc, ncs_c, bias_c, dstT) in enumerate(
                ((wk_b, ncsk_c, bk_c, kT), (wv_b, ncsv_c, bv_c, uT))):
            Px = PA if wi == 0 else PB
            Px.wait_r(p, "tensor")
            xb_b.wait_w(p, "tensor")
            if sub == 0 and wi == 0:
                p.wait("tensor", s_dve, m_cs_done)
            def mmproj(E, wsrc=wsrc, Px=Px):
                for cc in range(NCC):
                    i = E.matmul(Px.tile[:, 0:512], wsrc[:, cc * 128:(cc + 1) * 128],
                                 xb_b.tile[:, cc * 512:(cc + 1) * 512],
                                 start=(cc == 0), stop=(cc == NCC - 1))
                return i
            mmp = p.op("tensor", mmproj, s_pe)
            Px.w = (s_pe, mmp)
            if wi == 1:
                xb_b.r = (s_pe, mmp)
            Px.wait_w(p, "vector")
            p.wait("vector", s_dve, m_bc)
            p.op("vector", lambda E, Px=Px, ncs_c=ncs_c: E.scalar_tensor_tensor(
                ev1[:], mb_b[:], ncs_c, Px.tile[:, 0:512], ALU.mult, ALU.add))
            mev1 = p.op("vector", lambda E: E.tensor_mul(ev2[:], ev1[:], rb_b[:]), s_dve)
            Px.r = (s_dve, mev1)
            mev2 = p.opw("vector", lambda E, dstT=dstT, bias_c=bias_c, sub=sub: E.tensor_scalar(
                dstT[:, sub * 512:(sub + 1) * 512], ev2[:], bias_c, None, ALU.add), s_dve)
            if wi == 0:
                m_kT_w = mev2
            else:
                m_uT_w = mev2

        for i in range(4):
            kt = sub * 4 + i
            PT.wait_r(p, "tensor")
            p.wait("tensor", s_dve, m_uT_w)
            p.op("tensor", lambda E, kt=kt: E.transpose(
                PT.tile, uT[:, kt * 128:(kt + 1) * 128], ident), s_pe)
            PE_.wait_r(p, "tensor")
            PE_.wait_w(p, "tensor")
            mtr = pe_fence(PE_.tile[0:1, 511:512])
            PE_.w = (s_pe, mtr)
            PT.w = (s_pe, mtr)
            PT.wait_w(p, "scalar")
            muN = p.opw("scalar", lambda E, kt=kt: E.copy(uN[:, kt * 128:(kt + 1) * 128], PT.tile), s_act)
            PT.r = (s_act, muN)
            m_uN_w = muN

    if debug:
        p.wait("gpsimd", s_dve, m_kT_w)
        p.wait("gpsimd", s_act, m_uN_w)
        p.op("gpsimd", lambda E: E.dma_start(dbg["k"].ap().bitcast(BF16)[:, 0:SALL], kT[:]), s_gp, 16)
        p.op("gpsimd", lambda E: E.dma_start(dbg["u"].ap().bitcast(BF16)[:, 0:SALL], uT[:]), s_gp, 16)

    for nm in ("xb", "sq", "tr1", "tr2", "rb_b", "mb_b", "ev1", "ev2"):
        ar.free(nm)
    AF_ = ("vector", "scalar", "tensor")
    expT = Buf(AL("expT", 32 * 512 // 2, BF16, eng_fence=AF_), "expT")
    zacc = Buf(AL("zacc", 512, eng_fence=AF_), "zacc")
    obuf = Buf(AL("obuf", 256, BF16, eng_fence=AF_), "obuf")
    rz_b = AL("rz_b", 512, eng_fence=AF_)

    m_x1_store = []
    for qc in range(NQC):
        m_exp_tiles = []
        for kt in range(32):
            Px = PA if kt % 2 == 0 else PB
            Px.wait_r(p, "tensor")
            if qc == 0 and kt == 0:
                p.wait("tensor", s_dve, m_kT_w)
                p.wait("tensor", s_dve, m_uT_w)
                p.wait("tensor", s_act, m_uN_w)
                p.wait("tensor", s_act, m_wo)
                p.wait("vector", s_ldw, 16)
            if kt == 0:
                expT.wait_r(p, "tensor")
            mmx = p.op("tensor", lambda E, kt=kt, qc=qc, Px=Px: E.matmul(
                Px.tile[:, 0:512], kT[:, kt * 128:(kt + 1) * 128], uT[:, qc * 512:(qc + 1) * 512],
                start=True, stop=True), s_pe)
            Px.w = (s_pe, mmx)
            Px.wait_w(p, "scalar")
            if kt == 0:
                expT.wait_r(p, "scalar")
            mex = p.opw("scalar", lambda E, kt=kt, Px=Px: E.activation(
                expT.tile[:, kt * 512:(kt + 1) * 512], Px.tile[:, 0:512], AF.Exp, scale=1.0 / 128.0), s_act)
            Px.r = (s_act, mex)
            m_exp_tiles.append(mex)
        expT.w = (s_act, m_exp_tiles[-1])

        zacc.wait_r(p, "vector")
        p.wait("vector", s_act, m_exp_tiles[1])
        p.op("vector", lambda E: E.tensor_add(zacc.tile[:], expT.tile[:, 0:512], expT.tile[:, 512:1024]))
        mz = None
        for kt in range(2, 32):
            p.wait("vector", s_act, m_exp_tiles[kt])
            mz = p.op("vector", lambda E, kt=kt: E.tensor_add(
                zacc.tile[:], zacc.tile[:], expT.tile[:, kt * 512:(kt + 1) * 512]),
                s_dve if kt == 31 else None)
        zacc.w = (s_dve, mz)
        mzv = ones_colsum(zacc, PS1, stripA)
        if debug:
            p.wait("gpsimd", s_dve, mzv)
            mdz = p.op("gpsimd", lambda E, qc=qc: E.dma_start(dbg["z"].ap()[qc:qc + 1, :], stripA[0:1, 0:512]), s_gp, 16)
        p.wait("scalar", s_dve, mzv)
        if debug:
            p.wait("scalar", s_gp, mdz)
        mlz = p.op("scalar", lambda E: E.activation(stripA[0:1, 0:512], stripA[0:1, 0:512], AF.Ln), s_act)
        p.wait("scalar", s_act, mlz)
        mrz = p.op("scalar", lambda E: E.activation(stripA[0:1, 0:512], stripA[0:1, 0:512], AF.Exp, scale=-1.0), s_act)
        if debug:
            p.wait("gpsimd", s_act, mrz)
            mdrz = p.op("gpsimd", lambda E, qc=qc: E.dma_start(dbg["rz"].ap()[qc:qc + 1, :], stripA[0:1, 0:512]), s_gp, 16)
            p.wait("tensor", s_gp, mdrz)
        mrzb = bcast(stripA, rz_b, PC, s_act, mrz)

        PD.wait_r(p, "tensor")
        p.wait("tensor", s_act, m_exp_tiles[-1])
        def mmout(E):
            for kt in range(32):
                i = E.matmul(PD.tile[:, 0:512], uN[:, kt * 128:(kt + 1) * 128],
                             expT.tile[:, kt * 512:(kt + 1) * 512], start=(kt == 0), stop=(kt == 31))
            return i
        mmo = p.op("tensor", mmout, s_pe)
        PD.w = (s_pe, mmo)
        expT.r = (s_pe, mmo)
        if debug and qc == 0:
            p.wait("gpsimd", s_act, m_exp_tiles[-1])
            mde = p.op("gpsimd", lambda E: E.dma_start(
                dbg["expt"].ap().bitcast(BF16)[:, 0:32 * 512], expT.tile[:]), s_gp, 16)
        PD.wait_w(p, "vector")
        obuf.wait_r(p, "vector")
        p.wait("vector", s_dve, mrzb)
        mob = p.opw("vector", lambda E: E.tensor_mul(obuf.tile[:, 0:512], PD.tile[:, 0:512], rz_b[:]), s_dve)
        PD.r = (s_dve, mob)
        obuf.w = (s_dve, mob)
        if debug:
            p.wait("gpsimd", s_dve, mob)
            mdo = p.op("gpsimd", lambda E, qc=qc: E.dma_start(
                dbg["obuf"].ap()[qc].bitcast(BF16)[:, 0:512], obuf.tile[:, 0:512]), s_gp, 16)
            p.wait("tensor", s_gp, mdo)

        X_b = xst[qc % 2]
        X_b.wait_r(p, "scalar")
        mxl = p.op("scalar", lambda E, qc=qc, X_b=X_b: E.dma_start(
            X_b.tile.rearrange("p (cc t) -> p cc t", cc=NCC), xsub_ap(XT_OWN, qc)), s_ldx, 16)
        X_b.w = (s_ldx, mxl)
        mx1 = None
        for nt in range(16):
            Px = PA if nt % 2 == 0 else PB
            Px.wait_r(p, "tensor")
            obuf.wait_w(p, "tensor")
            mmy = p.op("tensor", lambda E, nt=nt, Px=Px: E.matmul(
                Px.tile[:, 0:512], wo_b[:, nt * 128:(nt + 1) * 128], obuf.tile[:, 0:512],
                start=True, stop=True), s_pe)
            Px.w = (s_pe, mmy)
            if nt == 15:
                obuf.r = (s_pe, mmy)
            Px.wait_w(p, "vector")
            X_b.wait_w(p, "vector")
            mx1 = p.opw("vector", lambda E, Px=Px, nt=nt, X_b=X_b: E.scalar_tensor_tensor(
                X_b.tile[:, nt * 512:(nt + 1) * 512], Px.tile[:, 0:512], bo_t[:, nt:nt + 1],
                X_b.tile[:, nt * 512:(nt + 1) * 512], ALU.add, ALU.add), s_dve)
            Px.r = (s_dve, mx1)
        p.wait("gpsimd", s_dve, mx1)
        ms = p.op("gpsimd", lambda E, qc=qc, X_b=X_b: E.dma_start(
            xsub_ap(X1_HBM, qc), X_b.tile.rearrange("p (cc t) -> p cc t", cc=NCC)), s_gp, 16)
        X_b.r = (s_gp, ms)
        m_x1_store.append(ms)

    if debug:
        p.wait("gpsimd", s_gp, m_x1_store[-1])
        p.op("gpsimd", lambda E: E.dma_start(dbg["x1"].ap(), X1_HBM.ap()), s_gp, 16)

    for nm in ("expT", "zacc", "obuf", "rz_b", "scan0", "scan1", "scan_acc",
               "scan_m", "xst0", "xst1", "wk_b", "wv_b", "wo_b", "kT", "uT", "uN"):
        ar.free(nm)

    snap_a1 = p.snap()
    CF = ("vector", "scalar", "tensor", "sync", "gpsimd")
    for _e in CF:
        p.wait_snap(_e, snap_a1)
    x1c = Buf(AL("x1c", NCC * 512, eng_fence=CF), "x1c")
    xqT = Buf(AL("xqT", NCC * 512 // 2, BF16, eng_fence=CF), "xqT")
    hbuf = Buf(AL("hbuf", NOT * 512 // 2, BF16, eng_fence=CF), "hbuf")
    ab_b = AL("ab_b", 512, eng_fence=CF)
    be_b = AL("be_b", 512, eng_fence=CF)
    ga_b = AL("ga_b", 512, eng_fence=CF)
    cb_b = AL("cb_b", 512, eng_fence=CF)
    c3_b = AL("c3_b", 512, eng_fence=CF)
    qint = Buf(AL("qint", 512, I32, eng_fence=CF), "qint")
    sil = Buf(AL("sil", 512, eng_fence=CF), "sil")
    outb = [Buf(AL(f"outb{i}", 512, eng_fence=CF), f"outb{i}") for i in range(2)]
    stripE = AL("stripE", 512, eng_fence=CF)
    stripF = AL("stripF", 512, eng_fence=CF)

    m_tern_gu = {}
    m_tern_d = {}

    for chunk in range(NCHUNK):
        CFc = ("vector", "scalar", "tensor", "sync", "gpsimd")
        sqf = Buf(AL("sqf", 2048, eng_fence=CFc), "sqf")
        trf1 = Buf(AL("trf1", 512, eng_fence=CFc), "trf1")
        trf2 = Buf(AL("trf2", 512, eng_fence=CFc), "trf2")
        mxt = Buf(AL("mxt", 512, eng_fence=CFc), "mxt")
        mnt = Buf(AL("mnt", 512, eng_fence=CFc), "mnt")

        x1c.wait_r(p, "scalar")
        p.wait("scalar", s_gp, m_x1_store[chunk])
        mxc = p.op("scalar", lambda E, chunk=chunk: E.dma_start(
            x1c.tile.rearrange("p (cc t) -> p cc t", cc=NCC), xsub_ap(X1_HBM, chunk)), s_ldx, 16)
        x1c.w = (s_ldx, mxc)

        x1c.wait_w(p, "vector")
        trf1.wait_r(p, "vector")
        def tree_sum2(E):
            E.tensor_add(trf1.tile[:], x1c.tile[:, 0:512], x1c.tile[:, 512:1024])
            for cc in range(2, NCC):
                i = E.tensor_add(trf1.tile[:], trf1.tile[:], x1c.tile[:, cc * 512:(cc + 1) * 512])
            return i
        mt1 = p.op("vector", tree_sum2, s_dve)
        trf1.w = (s_dve, mt1)
        trf2.wait_r(p, "vector")
        mt2 = None
        for qtr in range(4):
            sqf.wait_r(p, "scalar")
            x1c.wait_w(p, "scalar")
            msq = p.op("scalar", lambda E, qtr=qtr: E.square(
                sqf.tile[:], x1c.tile[:, qtr * 2048:(qtr + 1) * 2048]), s_act)
            sqf.w = (s_act, msq)
            p.wait("vector", s_act, msq)
            def tree_sq2(E, qtr=qtr):
                if qtr == 0:
                    E.tensor_add(trf2.tile[:], sqf.tile[:, 0:512], sqf.tile[:, 512:1024])
                    rng_ = range(2, 4)
                else:
                    rng_ = range(0, 4)
                for cc in rng_:
                    i = E.tensor_add(trf2.tile[:], trf2.tile[:], sqf.tile[:, cc * 512:(cc + 1) * 512])
                return i
            mt2 = p.op("vector", tree_sq2, s_dve)
            sqf.r = (s_dve, mt2)
        trf2.w = (s_dve, mt2)
        mxt.wait_r(p, "vector")
        mnt.wait_r(p, "vector")
        def tree_mx(E):
            E.tensor_tensor(mxt.tile[:], x1c.tile[:, 0:512], x1c.tile[:, 512:1024], ALU.max)
            E.tensor_tensor(mnt.tile[:], x1c.tile[:, 0:512], x1c.tile[:, 512:1024], ALU.min)
            for cc in range(2, NCC):
                E.tensor_tensor(mxt.tile[:], mxt.tile[:], x1c.tile[:, cc * 512:(cc + 1) * 512], ALU.max)
                i = E.tensor_tensor(mnt.tile[:], mnt.tile[:], x1c.tile[:, cc * 512:(cc + 1) * 512], ALU.min)
            return i
        mt3 = p.op("vector", tree_mx, s_dve)
        mxt.w = (s_dve, mt3)
        mnt.w = (s_dve, mt3)

        ones_colsum(trf1, PS1, stripA)
        ones_colsum(trf2, PS1, stripB)
        dmx = Buf(AL("dmx", 256, BF16, eng_fence=CFc, align=64), "dmx")
        dmn = Buf(AL("dmn", 256, BF16, eng_fence=CFc, align=64), "dmn")
        ttb = Buf(AL("ttb", 64, BF16, eng_fence=CFc), "ttb")
        mxt.wait_w(p, "scalar")
        mnt.wait_w(p, "scalar")
        dmx.wait_r(p, "scalar")
        dmn.wait_r(p, "scalar")
        p.op("scalar", lambda E: E.copy(dmx.tile[:], mxt.tile[:]))
        mcst = p.op("scalar", lambda E: E.copy(dmn.tile[:], mnt.tile[:]), s_act)
        dmx.w = (s_act, mcst)
        dmn.w = (s_act, mcst)
        mxt.r = (s_act, mcst)
        mnt.r = (s_act, mcst)
        mrd = None
        for di, acc in enumerate((dmx, dmn)):
            for i4 in range(4):
                ttb.wait_r(p, "scalar")
                acc.wait_w(p, "scalar")
                mtr = p.op("scalar", lambda E, acc=acc, i4=i4: E.dma_start(
                    ttb.tile[:], acc.tile[:, i4 * 128:(i4 + 1) * 128], transpose=True), s_ldx, 16)
                ttb.w = (s_ldx, mtr)
                ttb.wait_w(p, "vector")
                mrd = p.op("vector", lambda E, di=di, i4=i4: E.tensor_reduce(
                    tokv[:, di * 4 + i4: di * 4 + i4 + 1], ttb.tile[:], AX.X,
                    ALU.max if di == 0 else ALU.min), s_dve)
                ttb.r = (s_dve, mrd)
            acc.r = (s_ldx, mtr)
        bounce_strips(0, 4, stripC[0:1, 0:512], s_dve, mrd)
        mst_l = bounce_strips(4, 4, stripD[0:1, 0:512], s_dve, mrd)
        p.wait("vector", mst_l[0], mst_l[1])
        sA = stripA[0:1, 0:512]; sB = stripB[0:1, 0:512]
        sC = stripC[0:1, 0:512]; sD = stripD[0:1, 0:512]
        sE = stripE[0:1, 0:512]; sF = stripF[0:1, 0:512]
        p.vop(lambda E: E.tensor_scalar(sA, sA, 1.0 / D, None, ALU.mult), s_dve)
        p.vop(lambda E: E.tensor_scalar(sB, sB, 1.0 / D, EPS, ALU.mult, ALU.add), s_dve)
        p.vop(lambda E: E.scalar_tensor_tensor(sE, sA, -1.0, sA, ALU.mult, ALU.mult), s_dve)
        mvar = p.vop(lambda E: E.tensor_add(sB, sB, sE), s_dve)
        p.wait("scalar", s_dve, mvar)
        mln = p.op("scalar", lambda E: E.activation(sB, sB, AF.Ln), s_act)
        p.wait("scalar", s_act, mln)
        mr2 = p.op("scalar", lambda E: E.activation(sB, sB, AF.Exp, scale=-0.5), s_act)
        p.vop(lambda E: E.tensor_sub(sE, sC, sA), s_dve)
        p.vop(lambda E: E.tensor_sub(sF, sA, sD), s_dve)
        p.vop(lambda E: E.tensor_tensor(sE, sE, sF, ALU.max), s_dve)
        p.wait("vector", s_act, mr2)
        p.vop(lambda E: E.tensor_mul(sE, sE, sB), s_dve)
        mxa = p.vop(lambda E: E.tensor_scalar(sE, sE, EPS, None, ALU.max), s_dve)
        p.wait("scalar", s_dve, mxa)
        mln2 = p.op("scalar", lambda E: E.activation(sF, sE, AF.Ln), s_act)
        p.wait("scalar", s_act, mln2)
        mrcp = p.op("scalar", lambda E: E.activation(sF, sF, AF.Exp, scale=-1.0), s_act)
        p.wait("vector", s_act, mrcp)
        p.vop(lambda E: E.tensor_mul(sF, sB, sF), s_dve)
        ma_ = p.vop(lambda E: E.tensor_scalar(sF, sF, 127.0, None, ALU.mult), s_dve)
        mb_ = p.vop(lambda E: E.scalar_tensor_tensor(sD, sA, -1.0, sF, ALU.mult, ALU.mult), s_dve)
        mgm = p.vop(lambda E: E.tensor_scalar(sE, sE, 1.0 / 127.0, None, ALU.mult), s_dve)
        bcast(stripF, ab_b, PC, s_dve, ma_)
        bcast(stripD, be_b, PD, s_dve, mb_)
        mgb = bcast(stripE, ga_b, PE_, s_dve, mgm)

        mq = None
        for cc in range(NCC):
            qint.wait_r(p, "vector")
            xqT.wait_r(p, "vector")
            sil.wait_r(p, "vector")
            sil.wait_w(p, "vector")
            p.wait("vector", s_dve, mgb)
            p.op("vector", lambda E, cc=cc: E.tensor_mul(sil.tile[:], x1c.tile[:, cc * 512:(cc + 1) * 512], ab_b[:]))
            mqi = p.op("vector", lambda E: E.scalar_tensor_tensor(
                qint.tile[:], sil.tile[:], 512.0, be_b[:], ALU.add, ALU.add), s_dve)
            qint.w = (s_dve, mqi)
            mq = p.opw("vector", lambda E, cc=cc: E.scalar_tensor_tensor(
                xqT.tile[:, cc * 512:(cc + 1) * 512], qint.tile[:], -512.0, ga_b[:],
                ALU.add, ALU.mult), s_dve)
            qint.r = (s_dve, mq)
            sil.w = (s_dve, mq)
            sil.r = (s_dve, mq)
        xqT.w = (s_dve, mq)
        if debug and chunk == 0:
            p.wait("gpsimd", s_dve, mq)
            p.op("gpsimd", lambda E: E.dma_start(
                dbg["xq"].ap().bitcast(BF16)[:, 0:NCC * 512], xqT.tile[:]), s_gp, 16)

        for nm in ("sqf", "trf1", "trf2", "mxt", "mnt", "dmx", "dmn"):
            ar.free(nm)
        if chunk == 0:
            wgu = [Buf(AL(f"wgu{i}", 2 * D, eng_fence=CFc), f"wgu{i}") for i in range(2)]
            tpq0 = Buf(AL("tpq0", D // 2, BF16, eng_fence=CFc), "tpq0")
            tpq1 = Buf(AL("tpq1", D // 2, BF16, eng_fence=CFc), "tpq1")
        tg = [Buf(AL(f"tg{i}", D // 2, BF16, eng_fence=CFc), f"tg{i}") for i in range(2)]
        tu = [Buf(AL(f"tu{i}", D // 2, BF16, eng_fence=CFc), f"tu{i}") for i in range(2)]
        m_h = None
        for ot in range(NOT):
            par = ot % 2
            Tg, Tu = tg[par], tu[par]
            if chunk == 0:
                Wb = wgu[par]
                Wb.wait_r(p, "sync")
                p.op("sync", lambda E, ot=ot, Wb=Wb: E.dma_start(
                    Wb.tile[:, 0:D], WGU_T.ap()[ot, 0]), s_ldw, 16)
                mw = p.op("sync", lambda E, ot=ot, Wb=Wb: E.dma_start(
                    Wb.tile[:, D:2 * D], WGU_T.ap()[ot, 1]), s_ldw, 16)
                Wb.w = (s_ldw, mw)
                Wb.wait_w(p, "vector")
                for half, Tx, sc_p, sc_n in ((0, Tg, 0, 3), (1, Tu, 1, 4)):
                    Tx.wait_r(p, "vector")
                    tpq0.wait_r(p, "vector")
                    tpq1.wait_r(p, "vector")
                    p.wait("vector", s_dve, m_scales)
                    p.op("vector", lambda E, Wb=Wb, half=half, sc_p=sc_p: E.tensor_scalar(
                        tpq0.tile[:], Wb.tile[:, half * D:(half + 1) * D], scl[:, sc_p:sc_p + 1],
                        0.5, ALU.mult, ALU.is_ge))
                    mpq = p.op("vector", lambda E, Wb=Wb, half=half, sc_n=sc_n: E.tensor_scalar(
                        tpq1.tile[:], Wb.tile[:, half * D:(half + 1) * D], scl[:, sc_n:sc_n + 1],
                        0.5, ALU.mult, ALU.is_ge), s_dve)
                    tpq0.w = (s_dve, mpq)
                    tpq1.w = (s_dve, mpq)
                    if half == 1:
                        Wb.r = (s_dve, mpq)
                    mtn = p.opw("vector", lambda E, Tx=Tx: E.tensor_tensor(
                        Tx.tile[:], tpq0.tile[:], tpq1.tile[:], ALU.subtract), s_dve)
                    Tx.w = (s_dve, mtn)
                    tpq0.r = (s_dve, mtn)
                    tpq1.r = (s_dve, mtn)
                Tg.wait_w(p, "gpsimd")
                Tu.wait_w(p, "gpsimd")
                mst1 = p.op("gpsimd", lambda E, ot=ot, Tg=Tg: E.dma_start(
                    TGU_HBM.ap()[ot, 0], Tg.tile[:]), s_gp, 16)
                mst2 = p.op("gpsimd", lambda E, ot=ot, Tu=Tu: E.dma_start(
                    TGU_HBM.ap()[ot, 1], Tu.tile[:]), s_gp, 16)
                m_tern_gu[ot] = mst2
                if debug and chunk == 0 and ot == 0:
                    p.wait("gpsimd", s_ldw, Wb.w[1])
                    p.wait("gpsimd", s_dve, s_dve.n)
                    p.op("gpsimd", lambda E, Wb=Wb: E.dma_start(dbg["wgu0"].ap(), Wb.tile[:]), s_gp, 16)
                    p.op("gpsimd", lambda E: E.dma_start(
                        dbg["tp0"].ap().bitcast(BF16)[:, 0:D], tpq0.tile[:]), s_gp, 16)
                    p.op("gpsimd", lambda E: E.dma_start(
                        dbg["tq0"].ap().bitcast(BF16)[:, 0:D], tpq1.tile[:]), s_gp, 16)
                    p.op("gpsimd", lambda E, Tg=Tg: E.dma_start(
                        dbg["tg0"].ap().bitcast(BF16)[:, 0:D], Tg.tile[:]), s_gp, 16)
                    m_dbg0 = p.op("gpsimd", lambda E, Tu=Tu: E.dma_start(
                        dbg["tu0"].ap().bitcast(BF16)[:, 0:D], Tu.tile[:]), s_gp, 16)
                    p.wait("gpsimd", s_pe, s_pe.n)
                    m_dbg0 = p.op("gpsimd", lambda E, Wb=Wb: E.dma_start(dbg["wgu0c"].ap(), Wb.tile[:]), s_gp, 16)
                    Wb.r = (s_gp, m_dbg0)
                    tpq0.r = (s_gp, m_dbg0)
                    tpq1.r = (s_gp, m_dbg0)
                    p.wait("tensor", s_gp, m_dbg0)
            else:
                Tg.wait_r(p, "sync")
                Tu.wait_r(p, "sync")
                p.wait("sync", s_gp, m_tern_gu[ot])
                mw1 = p.op("sync", lambda E, ot=ot, Tg=Tg: E.dma_start(
                    Tg.tile[:], TGU_HBM.ap()[ot, 0]), s_ldw, 16)
                mw2 = p.op("sync", lambda E, ot=ot, Tu=Tu: E.dma_start(
                    Tu.tile[:], TGU_HBM.ap()[ot, 1]), s_ldw, 16)
                Tg.w = (s_ldw, mw1)
                Tu.w = (s_ldw, mw2)

            Pg = PA if par == 0 else PC
            Pu = PB if par == 0 else PD
            for (Px, Tx) in ((Pg, Tg), (Pu, Tu)):
                Px.wait_r(p, "tensor")
                Tx.wait_w(p, "tensor")
                if ot == 0:
                    xqT.wait_w(p, "tensor")
                def mmf(E, Px=Px, Tx=Tx):
                    for cc in range(NCC):
                        i = E.matmul(Px.tile[:, 0:512], Tx.tile[:, cc * 128:(cc + 1) * 128],
                                     xqT.tile[:, cc * 512:(cc + 1) * 512],
                                     start=(cc == 0), stop=(cc == NCC - 1))
                    return i
                mmx = p.op("tensor", mmf, s_pe)
                Px.w = (s_pe, mmx)
                Tx.r = (s_pe, mmx)
                if ot == NOT - 1 and Px is Pu:
                    xqT.r = (s_pe, mmx)
            Pg.wait_w(p, "scalar")
            sil.wait_r(p, "scalar")
            msl = p.opw("scalar", lambda E, Pg=Pg: E.activation(
                sil.tile[:], Pg.tile[:, 0:512], AF.Silu, scale=scl[:, 6:7]), s_act)
            Pg.r = (s_act, msl)
            sil.w = (s_act, msl)
            Pu.wait_w(p, "vector")
            sil.wait_w(p, "vector")
            hbuf.wait_r(p, "vector")
            m_h = p.opw("vector", lambda E, Pu=Pu, ot=ot: E.tensor_mul(
                hbuf.tile[:, ot * 512:(ot + 1) * 512], sil.tile[:], Pu.tile[:, 0:512]), s_dve)
            Pu.r = (s_dve, m_h)
            sil.r = (s_dve, m_h)
        hbuf.w = (s_dve, m_h)
        if debug and chunk == 0:
            p.wait("gpsimd", s_dve, m_h)
            mdh = p.op("gpsimd", lambda E: E.dma_start(
                dbg["h"].ap().bitcast(BF16)[:, 0:NOT * 512], hbuf.tile[:]), s_gp, 16)
            p.wait("vector", s_gp, mdh)

        mxh = Buf(AL("mxh", 256, BF16, eng_fence=CFc, align=64), "mxh")
        mnh = Buf(AL("mnh", 256, BF16, eng_fence=CFc, align=64), "mnh")
        mxh.wait_r(p, "vector")
        mnh.wait_r(p, "vector")
        def htree(E):
            E.tensor_tensor(mxh.tile[:], hbuf.tile[:, 0:512], hbuf.tile[:, 512:1024], ALU.max)
            E.tensor_tensor(mnh.tile[:], hbuf.tile[:, 0:512], hbuf.tile[:, 512:1024], ALU.min)
            for ot in range(2, NOT):
                E.tensor_tensor(mxh.tile[:], mxh.tile[:], hbuf.tile[:, ot * 512:(ot + 1) * 512], ALU.max)
                i = E.tensor_tensor(mnh.tile[:], mnh.tile[:], hbuf.tile[:, ot * 512:(ot + 1) * 512], ALU.min)
            return i
        mht = p.op("vector", htree, s_dve)
        mxh.w = (s_dve, mht)
        mnh.w = (s_dve, mht)
        mrd2 = None
        for di, acc in enumerate((mxh, mnh)):
            for i4 in range(4):
                ttb.wait_r(p, "scalar")
                acc.wait_w(p, "scalar")
                mtr = p.op("scalar", lambda E, acc=acc, i4=i4: E.dma_start(
                    ttb.tile[:], acc.tile[:, i4 * 128:(i4 + 1) * 128], transpose=True), s_ldx, 16)
                ttb.w = (s_ldx, mtr)
                ttb.wait_w(p, "vector")
                mrd2 = p.op("vector", lambda E, di=di, i4=i4: E.tensor_reduce(
                    tokv[:, 8 + di * 4 + i4: 8 + di * 4 + i4 + 1], ttb.tile[:], AX.X,
                    ALU.max if di == 0 else ALU.min), s_dve)
                ttb.r = (s_dve, mrd2)
            acc.r = (s_ldx, mtr)
        p.wait("vector", s_dve, mrd2)
        p.vop(lambda E: E.tensor_scalar(tokv[:, 12:16], tokv[:, 12:16], -1.0, None, ALU.mult), s_dve)
        mhb = p.vop(lambda E: E.tensor_tensor(tokv[:, 8:12], tokv[:, 8:12], tokv[:, 12:16], ALU.max), s_dve)
        msh_l = bounce_strips(8, 4, stripC[0:1, 0:512], s_dve, mhb)
        p.wait("vector", msh_l[0], msh_l[1])
        if debug and chunk == 0:
            p.wait("gpsimd", s_dve, s_dve.n)
            p.wait("gpsimd", s_ldx, s_ldx.n)
            p.op("gpsimd", lambda E: E.dma_start(dbg["tokv"].ap(), tokv[:]), s_gp, 16)
            p.op("gpsimd", lambda E: E.dma_start(dbg["mxh"].ap().bitcast(BF16)[:, 0:512], mxh.tile[:]), s_gp, 16)
            mdt = p.op("gpsimd", lambda E: E.dma_start(dbg["habs"].ap(), stripC[0:1, 0:512]), s_gp, 16)
            p.wait("vector", s_gp, mdt)
        mcl = p.vop(lambda E: E.tensor_scalar(sC, sC, scl[0:1, 9:10], EPS, ALU.mult, ALU.max), s_dve)
        mc3 = p.vop(lambda E: E.tensor_scalar(sD, sC, scl[0:1, 8:9], None, ALU.mult), s_dve)
        p.wait("scalar", s_dve, mcl)
        mlh = p.op("scalar", lambda E: E.activation(sE, sC, AF.Ln), s_act)
        p.wait("scalar", s_act, mlh)
        mrc = p.op("scalar", lambda E: E.activation(sE, sE, AF.Exp, scale=-1.0), s_act)
        p.wait("vector", s_act, mrc)
        mcb = p.vop(lambda E: E.tensor_scalar(sE, sE, scl[0:1, 7:8], None, ALU.mult), s_dve)
        mcbb = bcast(stripE, cb_b, PC, s_dve, mcb)
        mc3b = bcast(stripD, c3_b, PD, s_dve, mc3)
        if debug and chunk == 0:
            p.wait("gpsimd", s_dve, mc3b)
            mdc = p.op("gpsimd", lambda E: E.dma_start(dbg["cbb"].ap(), cb_b[:]), s_gp, 16)
            mdc = p.op("gpsimd", lambda E: E.dma_start(dbg["c3b"].ap(), c3_b[:]), s_gp, 16)
            p.wait("vector", s_gp, mdc)

        mhq = None
        for ot in range(NOT):
            sil.wait_r(p, "vector")
            sil.wait_w(p, "vector")
            p.wait("vector", s_dve, mc3b)
            p.op("vector", lambda E, ot=ot: E.tensor_mul(sil.tile[:], hbuf.tile[:, ot * 512:(ot + 1) * 512], cb_b[:]))
            qint.wait_r(p, "vector")
            mqi = p.op("vector", lambda E: E.tensor_scalar(qint.tile[:], sil.tile[:], 512.0, None, ALU.add), s_dve)
            qint.w = (s_dve, mqi)
            mhq = p.opw("vector", lambda E, ot=ot: E.tensor_scalar(
                hbuf.tile[:, ot * 512:(ot + 1) * 512], qint.tile[:], -512.0, None, ALU.add), s_dve)
            qint.r = (s_dve, mhq)
            sil.w = (s_dve, mhq)
            sil.r = (s_dve, mhq)
        hbuf.w = (s_dve, mhq)
        if debug and chunk == 0:
            p.wait("gpsimd", s_dve, mhq)
            p.op("gpsimd", lambda E: E.dma_start(
                dbg["hq"].ap().bitcast(BF16)[:, 0:NOT * 512], hbuf.tile[:]), s_gp, 16)

        for nm in ("mxh", "mnh", "tg0", "tg1", "tu0", "tu1"):
            ar.free(nm)
        if chunk == 0:
            ar.free("wgu0"); ar.free("wgu1")
        CF2 = ("vector", "scalar", "tensor", "sync", "gpsimd")
        if chunk == 0:
            wd_st = Buf(AL("wd_st", I_PAD, eng_fence=CF2), "wd_st")
        td = [Buf(AL(f"td{i}", I_PAD // 2, BF16, eng_fence=CF2), f"td{i}") for i in range(2)]

        mo = None
        for nt in range(16):
            par = nt % 2
            Td = td[par]
            if chunk == 0:
                wd_st.wait_r(p, "sync")
                mw = p.op("sync", lambda E, nt=nt: E.dma_start(wd_st.tile[:], WD_T.ap()[nt]), s_ldw, 16)
                wd_st.w = (s_ldw, mw)
                Td.wait_r(p, "vector")
                tpq0.wait_r(p, "vector")
                tpq1.wait_r(p, "vector")
                wd_st.wait_w(p, "vector")
                mtd = None
                for hh in range(4):
                    lo, hi = hh * (I_PAD // 4), (hh + 1) * (I_PAD // 4)
                    p.op("vector", lambda E, lo=lo, hi=hi: E.tensor_scalar(
                        tpq0.tile[:, 0:hi - lo], wd_st.tile[:, lo:hi], scl[:, 2:3], 0.5, ALU.mult, ALU.is_ge))
                    mpq = p.op("vector", lambda E, lo=lo, hi=hi: E.tensor_scalar(
                        tpq1.tile[:, 0:hi - lo], wd_st.tile[:, lo:hi], scl[:, 5:6], 0.5, ALU.mult, ALU.is_ge), s_dve)
                    mtd = p.opw("vector", lambda E, Td=Td, lo=lo, hi=hi: E.tensor_tensor(
                        Td.tile[:, lo:hi], tpq0.tile[:, 0:hi - lo], tpq1.tile[:, 0:hi - lo], ALU.subtract), s_dve)
                tpq0.r = (s_dve, mtd)
                tpq1.r = (s_dve, mtd)
                wd_st.r = (s_dve, mpq)
                Td.w = (s_dve, mtd)
                Td.wait_w(p, "gpsimd")
                mstd = p.op("gpsimd", lambda E, nt=nt, Td=Td: E.dma_start(
                    TD_HBM.ap()[nt], Td.tile[:]), s_gp, 16)
                m_tern_d[nt] = mstd
            else:
                Td.wait_r(p, "sync")
                p.wait("sync", s_gp, m_tern_d[nt])
                mw = p.op("sync", lambda E, nt=nt, Td=Td: E.dma_start(Td.tile[:], TD_HBM.ap()[nt]), s_ldw, 16)
                Td.w = (s_ldw, mw)

            Px = PA if par == 0 else PB
            Px.wait_r(p, "tensor")
            Td.wait_w(p, "tensor")
            if nt == 0:
                hbuf.wait_w(p, "tensor")
            def mmd(E, Px=Px, Td=Td):
                for ic in range(NOT):
                    i = E.matmul(Px.tile[:, 0:512], Td.tile[:, ic * 128:(ic + 1) * 128],
                                 hbuf.tile[:, ic * 512:(ic + 1) * 512],
                                 start=(ic == 0), stop=(ic == NOT - 1))
                return i
            mmx = p.op("tensor", mmd, s_pe)
            Px.w = (s_pe, mmx)
            Td.r = (s_pe, mmx)
            if nt == 15:
                hbuf.r = (s_pe, mmx)
            Ob = outb[par]
            Px.wait_w(p, "vector")
            Ob.wait_r(p, "vector")
            p.wait("vector", s_dve, mc3b)
            p.op("vector", lambda E, Px=Px, Ob=Ob: E.tensor_mul(Ob.tile[:], Px.tile[:, 0:512], c3_b[:]))
            mo = p.opw("vector", lambda E, Ob=Ob, nt=nt: E.tensor_add(
                Ob.tile[:], Ob.tile[:], x1c.tile[:, nt * 512:(nt + 1) * 512]), s_dve)
            Px.r = (s_dve, mo)
            Ob.w = (s_dve, mo)
            if nt == 15:
                x1c.r = (s_dve, mo)
            Ob.wait_w(p, "gpsimd")
            mso = p.op("gpsimd", lambda E, nt=nt, chunk=chunk, Ob=Ob: E.dma_start(
                bass.AP(OUT_T, nt * 128 * SOWN + chunk * 512, [[SOWN, 128], [1, 512]]),
                Ob.tile[:]), s_gp, 16)
            Ob.r = (s_gp, mso)
        if chunk == 0:
            ar.free("wd_st"); ar.free("tpq0"); ar.free("tpq1")
        ar.free("td0"); ar.free("td1"); ar.free("ttb")

    if debug:
        p.wait("gpsimd", s_ldw, s_ldw.n)
        p.wait("gpsimd", s_dve, s_dve.n)
        p.op("gpsimd", lambda E: E.dma_start(dbg["wgu0b"].ap(), ar.base[:, 33416:33416 + 4096]), s_gp, 16)
    p.wait("gpsimd", s_gp, s_gp.n)

    blk = nc.Block()
    blk.__enter__()
    p.emit(blk)
    blk.__exit__(None, None, None)
    for g in reversed(guards):
        g.__exit__(None, None, None)
    ar._g.__exit__(None, None, None)
    return nc


def host_prepare(inputs):
    f32 = np.float32
    x = np.ascontiguousarray(inputs["L0_attn_q"], f32).reshape(16384, D)
    xt = np.ascontiguousarray(x.T)
    wk = np.ascontiguousarray(np.asarray(inputs["L0_attn_Wk"], f32).reshape(NCC, 128, 128))
    wv = np.ascontiguousarray(np.asarray(inputs["L0_attn_Wv"], f32).reshape(NCC, 128, 128))
    bk = np.ascontiguousarray(np.asarray(inputs["L0_attn_bk"], f32).reshape(128, 1))
    bv = np.ascontiguousarray(np.asarray(inputs["L0_attn_bv"], f32).reshape(128, 1))
    wo = np.ascontiguousarray(np.asarray(inputs["L0_attn_Wo"], f32).reshape(16, 128, D))
    bo_t = np.ascontiguousarray(np.asarray(inputs["L0_attn_bo"], f32).reshape(16, 128).T)
    wg = np.asarray(inputs["L0_ffn_w_gate"], f32)
    wu = np.asarray(inputs["L0_ffn_w_up"], f32)
    wd = np.asarray(inputs["L0_ffn_w_down"], f32)
    wgp = np.zeros((I_PAD, D), f32); wgp[:I_RAW] = wg
    wup = np.zeros((I_PAD, D), f32); wup[:I_RAW] = wu
    wdp = np.zeros((D, I_PAD), f32); wdp[:, :I_RAW] = wd
    wg_t = np.transpose(wgp.reshape(NOT, 128, NCC, 128), (0, 3, 2, 1)).reshape(NOT, 128, D)
    wu_t = np.transpose(wup.reshape(NOT, 128, NCC, 128), (0, 3, 2, 1)).reshape(NOT, 128, D)
    wgu_t = np.ascontiguousarray(np.stack([wg_t, wu_t], axis=1))
    wd_t = np.ascontiguousarray(
        np.transpose(wdp.reshape(16, 128, NOT, 128), (0, 3, 2, 1)).reshape(16, 128, I_PAD))
    shared = dict(wk=wk, wv=wv, bk=bk, bv=bv, wo=wo, bo_t=bo_t, wgu_t=wgu_t, wd_t=wd_t)
    in_maps = []
    for c in range(8):
        nb = c ^ 1
        m = dict(shared)
        m["xt_own"] = np.ascontiguousarray(xt[:, c * SOWN:(c + 1) * SOWN])
        m["xt_nb"] = np.ascontiguousarray(xt[:, nb * SOWN:(nb + 1) * SOWN])
        in_maps.append(m)
    return in_maps


def host_finish(results):
    outs = [np.asarray(r["out_t"]) for r in results]
    full = np.concatenate(outs, axis=1)
    return np.ascontiguousarray(full.T).reshape(4, 4096, D)


from concourse.bass_utils import run_bass_kernel_spmd


def kernel(**inputs):
    nc = build(debug=0)
    in_maps = host_prepare(inputs)
    res = run_bass_kernel_spmd(nc, in_maps, core_ids=list(range(8)))
    return host_finish(res.results)


# version 04 (final)
# speedup vs baseline: 1.0288x; 1.0288x over previous
import numpy as np
import ml_dtypes
import concourse.bass as bass
import concourse.mybir as mybir

F32 = mybir.dt.float32
BF16 = mybir.dt.bfloat16
I32 = mybir.dt.int32
AF = mybir.ActivationFunctionType
ALU = mybir.AluOpType
AX = mybir.AxisListType

D = 2048
NCC = D // 128
SOWN = 2048
SALL = 4096
NSUB_KV = SALL // 512
NQC = SOWN // 512
I_RAW = 6464
I_PAD = 6528
NOT = I_PAD // 128
TC = 512
NCHUNK = SOWN // TC
W_COUNT = float(I_RAW * D)
EPS = 1e-5

ENGS = ("sync", "scalar", "vector", "tensor", "gpsimd")


class Sem:
    def __init__(self, nc, name):
        self._g = nc.semaphore(name)
        self.h = self._g.__enter__()
        self.n = 0

    def next(self, k=1):
        self.n += k
        return self.n


class Prog:
    def __init__(self, nc, sems):
        self.nc = nc
        self.q = {e: [] for e in ENGS}
        self.sems = sems

    def on(self, eng, fn):
        self.q[eng].append(fn)

    def op(self, eng, fn, sem=None, k=1):
        if sem is None:
            self.on(eng, fn)
            return None
        m = sem.next(k)
        self.on(eng, lambda E, fn=fn, sem=sem, k=k: fn(E).then_inc(sem.h, k))
        return m

    def opw(self, eng, fn, sem, k=1):
        m = self.op(eng, fn, sem, k)
        self.wait(eng, sem, m)
        return m

    def vop(self, fn, sem):
        m = self.op("vector", fn, sem)
        self.wait("vector", sem, m)
        return m

    def wait(self, eng, sem, milestone):
        if milestone is None or milestone <= 0:
            return
        assert milestone <= sem.n, f"future milestone {milestone} > {sem.n}"
        self.on(eng, lambda E, s=sem, m=milestone: E.wait_ge(s.h, m))

    def snap(self):
        return {s: s.n for s in self.sems}

    def wait_snap(self, eng, snap):
        for s, n in snap.items():
            if n > 0:
                self.wait(eng, s, n)

    def emit(self, blk):
        for e in ENGS:
            fns = self.q[e]
            def body(E, fns=fns):
                for f in fns:
                    f(E)
            getattr(blk, e)(body)


class Buf:
    def __init__(self, tile, name=""):
        self.tile = tile
        self.name = name
        self.w = None
        self.r = None

    def wait_w(self, p, eng):
        if self.w is not None:
            p.wait(eng, self.w[0], self.w[1])

    def wait_r(self, p, eng):
        if self.r is not None:
            p.wait(eng, self.r[0], self.r[1])


class Arena:
    def __init__(self, nc, p, total_f32):
        self._g = nc.sbuf_tensor("arena", [128, total_f32], F32)
        self.base = self._g.__enter__()
        self.base_bf = self.base.tensor.bitcast(BF16) if hasattr(self.base, "tensor") else None
        self.nc = nc
        self.p = p
        self.total = total_f32
        self.free_list = [[0, total_f32, None]]
        self.regions = {}

    def _view(self, st, n, dtype):
        if dtype == F32:
            return self.base[:, st:st + n]
        if dtype == BF16:
            return self.base.bitcast(BF16)[:, 2 * st:2 * (st + n)]
        if dtype == I32:
            return self.base.bitcast(I32)[:, st:st + n]
        raise ValueError(dtype)

    def alloc(self, name, n_f32, dtype=F32, align=4):
        n_f32 = (n_f32 + 3) & ~3
        for i, seg in enumerate(self.free_list):
            st, ln, fence = seg
            pad = (-st) % align
            if ln >= n_f32 + pad:
                if pad:
                    self.free_list[i] = [st + pad, ln - pad, fence]
                    st, ln = st + pad, ln - pad
                    seg = self.free_list[i]
                seg[0] = st + n_f32
                seg[1] = ln - n_f32
                if seg[1] == 0:
                    self.free_list.pop(i)
                self.regions[name] = (st, n_f32)
                return self._view(st, n_f32, dtype), fence
        raise RuntimeError(f"arena OOM: {name} needs {n_f32 * 4}B; free={[(s[0], s[1]) for s in self.free_list]}")

    @staticmethod
    def _merge_fence(a, b):
        if a is None:
            return dict(b) if b else None
        if b is None:
            return dict(a)
        out = dict(a)
        for s, n in b.items():
            out[s] = max(out.get(s, 0), n)
        return out

    def free(self, name):
        st, ln = self.regions.pop(name)
        self.free_list.append([st, ln, self.p.snap()])
        self.free_list.sort(key=lambda s: s[0])
        merged = []
        for seg in self.free_list:
            if merged and merged[-1][0] + merged[-1][1] == seg[0]:
                merged[-1][1] += seg[1]
                merged[-1][2] = self._merge_fence(merged[-1][2], seg[2])
            else:
                merged.append(seg)
        self.free_list = merged


def build(debug=0):
    nc = bass.Bass("TRN2", target_bir_lowering=False, detect_race_conditions=False)

    XT_OWN = nc.dram_tensor("xt_own", [D, SOWN], F32, kind="ExternalInput")
    XT_NB = nc.dram_tensor("xt_nb", [D, SOWN], F32, kind="ExternalInput")
    WK = nc.dram_tensor("wk", [NCC, 128, 128], F32, kind="ExternalInput")
    WV = nc.dram_tensor("wv", [NCC, 128, 128], F32, kind="ExternalInput")
    BK = nc.dram_tensor("bk", [128, 1], F32, kind="ExternalInput")
    BV = nc.dram_tensor("bv", [128, 1], F32, kind="ExternalInput")
    WO = nc.dram_tensor("wo", [16, 128, D], F32, kind="ExternalInput")
    BO_T = nc.dram_tensor("bo_t", [128, 16], F32, kind="ExternalInput")
    WGU_T = nc.dram_tensor("wgu_t", [NOT, 2, 128, D], F32, kind="ExternalInput")
    WD_T = nc.dram_tensor("wd_t", [16, 128, I_PAD], F32, kind="ExternalInput")
    OUT_T = nc.dram_tensor("out_t", [D, SOWN], F32, kind="ExternalOutput")

    X1_HBM = nc.dram_tensor("x1_scratch", [D, SOWN], F32, kind="Internal")
    TOK_HBM = nc.dram_tensor("tok_scratch", [128, 16], F32, kind="Internal")
    TGU_HBM = nc.dram_tensor("tgu_c", [NOT, 2, 128, D], BF16, kind="Internal")
    TD_HBM = nc.dram_tensor("td_c", [16, 128, I_PAD], BF16, kind="Internal")

    dbg = {}
    if debug:
        for nm, shp in (("k", [128, SALL]), ("u", [128, SALL]), ("x1", [D, SOWN]),
                        ("xq", [128, NCC * 512]), ("hq", [128, NOT * 512]), ("scl", [128, 16]),
                        ("z", [4, 512]), ("rz", [4, 512]), ("expt", [128, 32 * 512]),
                        ("obuf", [4, 128, 512]), ("h", [128, NOT * 512]), ("tg0", [128, D]),
                        ("wgu0", [128, 2 * D]), ("wgu0b", [128, 2 * D]), ("wgu0c", [128, 2 * D]), ("tp0", [128, D]), ("tq0", [128, D]),
                        ("tokv", [128, 16]), ("habs", [1, 512]), ("mxh", [128, 512]),
                        ("pt0", [128, 128]),
                        ("tu0", [128, D]), ("cbb", [128, 512]), ("c3b", [128, 512])):
            dbg[nm] = nc.dram_tensor("dbg_" + nm, shp, F32, kind="ExternalOutput")

    IDENT = nc.inline_tensor(np.eye(128, dtype=ml_dtypes.bfloat16), "c_ident")
    IDENT32 = nc.inline_tensor(np.eye(128, dtype=np.float32), "c_ident32")
    ONESC32 = nc.inline_tensor(np.ones((128, 1), dtype=np.float32), "c_onesc32")
    ONESCB = nc.inline_tensor(np.ones((128, 1), dtype=ml_dtypes.bfloat16), "c_onescb")
    ONESR32 = nc.inline_tensor(np.ones((1, 128), dtype=np.float32), "c_onesr32")

    s_ldw = Sem(nc, "s_ldw")
    s_ldx = Sem(nc, "s_ldx")
    s_gp = Sem(nc, "s_gp")
    s_pe = Sem(nc, "s_pe")
    s_dve = Sem(nc, "s_dve")
    s_act = Sem(nc, "s_act")
    SEMS = [s_ldw, s_ldx, s_gp, s_pe, s_dve, s_act]

    p = Prog(nc, SEMS)
    guards = []

    def psum(name, shape, dtype=F32):
        g = nc.psum_tensor(name, shape, dtype)
        t = g.__enter__()
        guards.append(g)
        return Buf(t[:], name)

    ar = Arena(nc, p, 52000)

    def AL(name, n, dtype=F32, eng_fence=(), align=4):
        view, fence = ar.alloc(name, n, dtype, align)
        if fence is not None:
            for e in eng_fence:
                p.wait_snap(e, fence)
        return view

    ident = AL("ident", 64, BF16)
    ident32 = AL("ident32", 128)
    onesc32 = AL("onesc32", 4)[:, 0:1]
    onescb = AL("onescb", 4, BF16)[:, 0:2][:, 0:1]
    onesr32 = AL("onesr32", 128)[0:1, 0:128]
    bk_c = AL("bk_c", 4)[:, 0:1]
    bv_c = AL("bv_c", 4)[:, 0:1]
    bo_t = AL("bo_t", 16)
    ncsk_c = AL("ncsk_c", 4)[:, 0:1]
    ncsv_c = AL("ncsv_c", 4)[:, 0:1]
    scl = AL("scl", 16)
    stripA = AL("stripA", 512)
    stripB = AL("stripB", 512)
    stripC = AL("stripC", 512)
    stripD = AL("stripD", 512)
    tokv = AL("tokv", 16)


    wk_b = AL("wk_b", D // 2, BF16)
    wv_b = AL("wv_b", D // 2, BF16)
    wo_b = AL("wo_b", D // 2, BF16)
    kT = AL("kT", SALL // 2, BF16)
    uT = AL("uT", SALL // 2, BF16)
    uN = AL("uN", SALL // 2, BF16)

    PA = psum("psA", [128, 512]); PB = psum("psB", [128, 512])
    PC = psum("psC", [128, 512]); PD = psum("psD", [128, 512])
    PE_ = psum("psE", [128, 512])
    PS1 = psum("psS1", [1, 512]); PS2 = psum("psS2", [1, 512])
    PT = psum("psT", [128, 128], BF16)

    def pe_fence(cell):
        m = p.op("tensor", lambda E, cell=cell: E.matmul(cell, onesr32[0:1, 0:1],
                                                         onesr32[0:1, 0:1], start=True, stop=True), s_pe)
        return m

    def ones_colsum(acc_buf, PSx, dst_strip, extra_dep=None):
        PSx.wait_r(p, "tensor")
        acc_buf.wait_w(p, "tensor")
        if extra_dep:
            p.wait("tensor", extra_dep[0], extra_dep[1])
        mmx = p.op("tensor", lambda E: E.matmul(PSx.tile[0:1, 0:512], onesc32, acc_buf.tile[:],
                                                start=True, stop=True), s_pe)
        PSx.w = (s_pe, mmx)
        acc_buf.r = (s_pe, mmx)
        PSx.wait_w(p, "vector")
        mev = p.opw("vector", lambda E: E.tensor_copy(dst_strip[0:1, 0:512], PSx.tile[0:1, 0:512]), s_dve)
        PSx.r = (s_dve, mev)
        return mev

    def bcast(strip_src, dst_tile, PSx, dep_sem, dep_m, n=512):
        PSx.wait_r(p, "tensor")
        p.wait("tensor", dep_sem, dep_m)
        mmx = p.op("tensor", lambda E: E.matmul(PSx.tile[:, 0:n], onesr32, strip_src[0:1, 0:n],
                                                start=True, stop=True), s_pe)
        PSx.w = (s_pe, mmx)
        PSx.wait_w(p, "vector")
        mev = p.opw("vector", lambda E: E.tensor_copy(dst_tile[:, 0:n], PSx.tile[:, 0:n]), s_dve)
        PSx.r = (s_dve, mev)
        return mev

    tokbuf = Buf(None, "tokbuf")

    def bounce_strips(col0, k, dst_strip_ap, dep_sem, dep_m):
        tokbuf.wait_r(p, "gpsimd")
        tokbuf.wait_w(p, "gpsimd")
        p.wait("gpsimd", dep_sem, dep_m)
        mo = p.op("gpsimd", lambda E: E.dma_start(
            bass.AP(TOK_HBM, 0, [[16, 128], [1, k]]), tokv[:, col0:col0 + k]), s_gp, 16)
        tokbuf.w = (s_gp, mo)
        p.wait("gpsimd", s_gp, mo)
        mb = p.op("gpsimd", lambda E: E.dma_start(
            dst_strip_ap, bass.AP(TOK_HBM, 0, [[1, 1], [1, k], [16, 128]]),
            allow_slow_non_contiguous=True), s_gp, 16)
        tokbuf.r = (s_gp, mb)
        return (s_gp, mb)

    for (dst, src) in ((ident, IDENT), (ident32, IDENT32), (onesc32, ONESC32),
                       (onescb, ONESCB), (bk_c, BK), (bv_c, BV), (bo_t, BO_T)):
        p.op("sync", lambda E, dst=dst, src=src: E.dma_start(dst, src.ap()), s_ldw, 16)
    p.op("sync", lambda E: E.dma_start(onesr32, ONESR32.ap()), s_ldw, 16)

    wkv32 = Buf(AL("wkv32", 2 * D), "wkv32")
    p.op("sync", lambda E: E.dma_start(
        wkv32.tile[:, 0:D].rearrange("p (cc d) -> p cc d", cc=NCC), WK.ap().transpose([1, 0, 2])), s_ldw, 16)
    p.op("sync", lambda E: E.dma_start(
        wkv32.tile[:, D:2 * D].rearrange("p (cc d) -> p cc d", cc=NCC), WV.ap().transpose([1, 0, 2])), s_ldw, 16)
    p.wait("scalar", s_ldw, s_ldw.n)
    p.op("scalar", lambda E: E.copy(wk_b[:], wkv32.tile[:, 0:D]), s_act)
    m_wkv_cast = p.opw("scalar", lambda E: E.copy(wv_b[:], wkv32.tile[:, D:2 * D]), s_act)

    def build_cs(src_b, out_col):
        PS1.wait_r(p, "tensor")
        p.wait("tensor", s_act, m_wkv_cast)
        def mm(E, src_b=src_b):
            for cc in range(NCC):
                i = E.matmul(PS1.tile[0:1, 0:128], onescb, src_b[:, cc * 128:(cc + 1) * 128],
                             start=(cc == 0), stop=(cc == NCC - 1))
            return i
        mm1 = p.op("tensor", mm, s_pe)
        PS1.w = (s_pe, mm1)
        PS1.wait_w(p, "vector")
        mneg = p.op("vector", lambda E: E.tensor_scalar(stripA[0:1, 0:128], PS1.tile[0:1, 0:128],
                                                        -1.0, None, ALU.mult), s_dve)
        PS1.r = (s_dve, mneg)
        PE_.wait_r(p, "tensor")
        p.wait("tensor", s_dve, mneg)
        p.op("tensor", lambda E: E.transpose(PE_.tile[:, 0:1], stripA[0:1, 0:128], ident32[0:1, 0:1]), s_pe)
        mtr = pe_fence(PE_.tile[0:1, 511:512])
        PE_.w = (s_pe, mtr)
        PE_.wait_w(p, "vector")
        mc = p.op("vector", lambda E, out_col=out_col: E.tensor_copy(out_col, PE_.tile[:, 0:1]), s_dve)
        PE_.r = (s_dve, mc)
        return mc

    build_cs(wk_b, ncsk_c)
    m_cs_done = build_cs(wv_b, ncsv_c)

    wo_st = [Buf(AL(f"wo_st{i}", D), f"wo_st{i}") for i in range(2)]
    wo_acc = Buf(AL("wo_acc", D), "wo_acc")
    mr = None
    for h in range(16):
        Wb = wo_st[h % 2]
        Wb.wait_r(p, "sync")
        mw = p.op("sync", lambda E, h=h, Wb=Wb: E.dma_start(Wb.tile[:], WO.ap()[h]), s_ldw, 16)
        Wb.w = (s_ldw, mw)
        Wb.wait_w(p, "vector")
        if h == 0:
            mr = p.op("vector", lambda E, Wb=Wb: E.tensor_copy(wo_acc.tile[:], Wb.tile[:]), s_dve)
        else:
            mr = p.op("vector", lambda E, Wb=Wb: E.tensor_add(wo_acc.tile[:], wo_acc.tile[:], Wb.tile[:]), s_dve)
        Wb.r = (s_dve, mr)
    p.wait("scalar", s_dve, mr)
    m_wo = p.opw("scalar", lambda E: E.copy(wo_b[:], wo_acc.tile[:]), s_act)
    ar.free("wkv32"); ar.free("wo_st0"); ar.free("wo_st1"); ar.free("wo_acc")

    CH_F = 3264
    scan_st = [Buf(AL(f"scan{i}", CH_F, eng_fence=("sync",)), f"scan{i}") for i in range(2)]
    scan_acc = AL("scan_acc", 136)
    scan_m = AL("scan_m", 4)
    scan_plan = []
    for ot in range(NOT):
        scan_plan.append((WGU_T.ap()[ot, 0], D, ot))
        scan_plan.append((WGU_T.ap()[ot, 1], D, 51 + ot))
    for nt in range(16):
        for hh in range(2):
            scan_plan.append((WD_T.ap()[nt][:, hh * 3264:(hh + 1) * 3264], 3264, 102 + 2 * nt + hh))
    scan_state = {"i": 0, "ma": None}

    def emit_scan(k):
        for _ in range(k):
            i = scan_state["i"]
            if i >= len(scan_plan):
                return
            scan_state["i"] = i + 1
            apv, width, col = scan_plan[i]
            Sb_ = scan_st[i % 2]
            Sb_.wait_r(p, "sync")
            mw = p.op("sync", lambda E, Sb_=Sb_, apv=apv, width=width: E.dma_start(
                Sb_.tile[:, 0:width], apv), s_ldw, 16)
            Sb_.w = (s_ldw, mw)
            Sb_.wait_w(p, "scalar")
            ma = p.op("scalar", lambda E, Sb_=Sb_, col=col, width=width: E.activation(
                Sb_.tile[:, 0:width], Sb_.tile[:, 0:width], AF.Abs, scale=1.0,
                accum_out=scan_acc[:, col:col + 1]), s_act)
            Sb_.r = (s_act, ma)
            scan_state["ma"] = ma

    emit_scan(2)
    ma = None
    def finish_scan():
        emit_scan(len(scan_plan))
        m_scan_done = scan_state["ma"]
        p.wait("vector", s_act, m_scan_done)
        mrd = None
        for mi, (lo, hi) in enumerate(((0, 51), (51, 102), (102, 134))):
            mrd = p.op("vector", lambda E, mi=mi, lo=lo, hi=hi: E.tensor_reduce(
                scan_m[:, mi:mi + 1], scan_acc[:, lo:hi], AX.X, ALU.add), s_dve)
        PS2.wait_r(p, "tensor")
        p.wait("tensor", s_dve, mrd)
        mms = p.op("tensor", lambda E: E.matmul(PS2.tile[0:1, 0:3], onesc32, scan_m[:, 0:3],
                                                start=True, stop=True), s_pe)
        PS2.w = (s_pe, mms)
        PS2.wait_w(p, "vector")
        mcm = p.op("vector", lambda E: E.tensor_scalar(stripD[0:1, 8:11], PS2.tile[0:1, 0:3],
                                                       1.0 / W_COUNT, EPS, ALU.mult, ALU.max), s_dve)
        PS2.r = (s_dve, mcm)
        PD.wait_r(p, "tensor")
        p.wait("tensor", s_dve, mcm)
        mbc2 = p.op("tensor", lambda E: E.matmul(PD.tile[:, 0:3], onesr32, stripD[0:1, 8:11],
                                                 start=True, stop=True), s_pe)
        PD.w = (s_pe, mbc2)
        PD.wait_w(p, "vector")
        p.vop(lambda E: E.tensor_copy(scl[:, 6:7], PD.tile[:, 0:1]), s_dve)
        p.vop(lambda E: E.tensor_scalar(scl[:, 7:8], PD.tile[:, 1:2], 127.0, None, ALU.mult), s_dve)
        p.vop(lambda E: E.tensor_scalar(scl[:, 8:9], PD.tile[:, 2:3], 1.0 / 127.0, None, ALU.mult), s_dve)
        p.vop(lambda E: E.tensor_copy(scl[:, 9:10], PD.tile[:, 1:2]), s_dve)
        mg3 = p.op("vector", lambda E: E.reciprocal(scl[:, 0:3], PD.tile[:, 0:3]), s_dve)
        PD.r = (s_dve, mg3)
        p.wait("vector", s_dve, mg3)
        m_scales = p.op("vector", lambda E: E.tensor_scalar(scl[:, 3:6], scl[:, 0:3], -1.0, None, ALU.mult), s_dve)
        return m_scales


    xst = [Buf(AL(f"xst{i}", NCC * 512), f"xst{i}") for i in range(2)]
    xb_b = Buf(AL("xb", NCC * 512 // 2, BF16), "xb")
    sq_b = Buf(AL("sq", 2048), "sq")
    tr1 = Buf(AL("tr1", 512), "tr1")
    tr2 = Buf(AL("tr2", 512), "tr2")
    rb_b = AL("rb_b", 512)
    mb_b = AL("mb_b", 512)
    ev1 = AL("ev1", 512)
    ev2 = AL("ev2", 512)

    def xsub_ap(src, sub):
        return bass.AP(src, sub * 512, [[SOWN, 128], [128 * SOWN, NCC], [1, 512]])

    m_kT_w = m_uT_w = m_uN_w = None
    for sub in range(NSUB_KV):
        src = XT_OWN if sub < 4 else XT_NB
        lsub = sub % 4
        X_b = xst[sub % 2]
        X_b.wait_r(p, "scalar")
        mx = p.op("scalar", lambda E, X_b=X_b, src=src, lsub=lsub: E.dma_start(
            X_b.tile.rearrange("p (cc t) -> p cc t", cc=NCC), xsub_ap(src, lsub)), s_ldx, 16)
        X_b.w = (s_ldx, mx)

        X_b.wait_w(p, "scalar")
        xb_b.wait_r(p, "scalar")
        mcast = p.opw("scalar", lambda E, X_b=X_b: E.copy(xb_b.tile[:], X_b.tile[:]), s_act)
        xb_b.w = (s_act, mcast)

        X_b.wait_w(p, "vector")
        tr1.wait_r(p, "vector")
        def tree_sum(E, X_b=X_b):
            E.tensor_add(tr1.tile[:], X_b.tile[:, 0:512], X_b.tile[:, 512:1024])
            for cc in range(2, NCC):
                i = E.tensor_add(tr1.tile[:], tr1.tile[:], X_b.tile[:, cc * 512:(cc + 1) * 512])
            return i
        mt1 = p.op("vector", tree_sum, s_dve)
        tr1.w = (s_dve, mt1)
        tr2.wait_r(p, "vector")
        mt2 = None
        for qtr in range(4):
            sq_b.wait_r(p, "scalar")
            X_b.wait_w(p, "scalar")
            msq = p.op("scalar", lambda E, X_b=X_b, qtr=qtr: E.square(
                sq_b.tile[:], X_b.tile[:, qtr * 2048:(qtr + 1) * 2048]), s_act)
            sq_b.w = (s_act, msq)
            p.wait("vector", s_act, msq)
            def tree_sq(E, qtr=qtr):
                if qtr == 0:
                    E.tensor_add(tr2.tile[:], sq_b.tile[:, 0:512], sq_b.tile[:, 512:1024])
                    rng_ = range(2, 4)
                else:
                    rng_ = range(0, 4)
                for cc in rng_:
                    i = E.tensor_add(tr2.tile[:], tr2.tile[:], sq_b.tile[:, cc * 512:(cc + 1) * 512])
                return i
            mt2 = p.op("vector", tree_sq, s_dve)
            sq_b.r = (s_dve, mt2)
        tr2.w = (s_dve, mt2)
        X_b.r = (s_dve, mt2)

        ones_colsum(tr1, PS1, stripA)
        mss = ones_colsum(tr2, PS2, stripB)
        p.vop(lambda E: E.tensor_scalar(stripA[0:1, 0:512], stripA[0:1, 0:512], 1.0 / D, None, ALU.mult), s_dve)
        p.vop(lambda E: E.tensor_scalar(stripB[0:1, 0:512], stripB[0:1, 0:512], 1.0 / D, EPS, ALU.mult, ALU.add), s_dve)
        p.vop(lambda E: E.tensor_mul(stripC[0:1, 0:512], stripA[0:1, 0:512], stripA[0:1, 0:512]), s_dve)
        mvar = p.vop(lambda E: E.tensor_sub(stripB[0:1, 0:512], stripB[0:1, 0:512], stripC[0:1, 0:512]), s_dve)
        p.wait("scalar", s_dve, mvar)
        mla = p.op("scalar", lambda E: E.activation(stripB[0:1, 0:512], stripB[0:1, 0:512], AF.Ln), s_act)
        p.wait("scalar", s_act, mla)
        mr_ = p.op("scalar", lambda E: E.activation(stripB[0:1, 0:512], stripB[0:1, 0:512], AF.Exp, scale=-0.5), s_act)
        bcast(stripA, mb_b, PC, s_dve, mvar)
        m_bc = bcast(stripB, rb_b, PD, s_act, mr_)

        for wi, (wsrc, ncs_c, bias_c, dstT) in enumerate(
                ((wk_b, ncsk_c, bk_c, kT), (wv_b, ncsv_c, bv_c, uT))):
            Px = PA if wi == 0 else PB
            Px.wait_r(p, "tensor")
            xb_b.wait_w(p, "tensor")
            if sub == 0 and wi == 0:
                p.wait("tensor", s_dve, m_cs_done)
            def mmproj(E, wsrc=wsrc, Px=Px):
                for cc in range(NCC):
                    i = E.matmul(Px.tile[:, 0:512], wsrc[:, cc * 128:(cc + 1) * 128],
                                 xb_b.tile[:, cc * 512:(cc + 1) * 512],
                                 start=(cc == 0), stop=(cc == NCC - 1))
                return i
            mmp = p.op("tensor", mmproj, s_pe)
            Px.w = (s_pe, mmp)
            if wi == 1:
                xb_b.r = (s_pe, mmp)
            Px.wait_w(p, "vector")
            p.wait("vector", s_dve, m_bc)
            p.op("vector", lambda E, Px=Px, ncs_c=ncs_c: E.scalar_tensor_tensor(
                ev1[:], mb_b[:], ncs_c, Px.tile[:, 0:512], ALU.mult, ALU.add))
            mev1 = p.op("vector", lambda E: E.tensor_mul(ev2[:], ev1[:], rb_b[:]), s_dve)
            Px.r = (s_dve, mev1)
            mev2 = p.opw("vector", lambda E, dstT=dstT, bias_c=bias_c, sub=sub: E.tensor_scalar(
                dstT[:, sub * 512:(sub + 1) * 512], ev2[:], bias_c, None, ALU.add), s_dve)
            if wi == 0:
                m_kT_w = mev2
            else:
                m_uT_w = mev2

        emit_scan(8)
        for i in range(4):
            kt = sub * 4 + i
            PT.wait_r(p, "tensor")
            p.wait("tensor", s_dve, m_uT_w)
            p.op("tensor", lambda E, kt=kt: E.transpose(
                PT.tile, uT[:, kt * 128:(kt + 1) * 128], ident), s_pe)
            PE_.wait_r(p, "tensor")
            PE_.wait_w(p, "tensor")
            mtr = pe_fence(PE_.tile[0:1, 511:512])
            PE_.w = (s_pe, mtr)
            PT.w = (s_pe, mtr)
            PT.wait_w(p, "scalar")
            muN = p.opw("scalar", lambda E, kt=kt: E.copy(uN[:, kt * 128:(kt + 1) * 128], PT.tile), s_act)
            PT.r = (s_act, muN)
            m_uN_w = muN

    if debug:
        p.wait("gpsimd", s_dve, m_kT_w)
        p.wait("gpsimd", s_act, m_uN_w)
        p.op("gpsimd", lambda E: E.dma_start(dbg["k"].ap().bitcast(BF16)[:, 0:SALL], kT[:]), s_gp, 16)
        p.op("gpsimd", lambda E: E.dma_start(dbg["u"].ap().bitcast(BF16)[:, 0:SALL], uT[:]), s_gp, 16)

    for nm in ("xb", "sq", "tr1", "tr2", "rb_b", "mb_b", "ev1", "ev2"):
        ar.free(nm)
    AF_ = ("vector", "scalar", "tensor")
    expT = Buf(AL("expT", 32 * 512 // 2, BF16, eng_fence=AF_), "expT")
    zacc = Buf(AL("zacc", 512, eng_fence=AF_), "zacc")
    obuf = Buf(AL("obuf", 256, BF16, eng_fence=AF_), "obuf")
    rz_b = AL("rz_b", 512, eng_fence=AF_)

    m_x1_store = []
    for qc in range(NQC):
        m_exp_tiles = []
        for kt in range(32):
            Px = PA if kt % 2 == 0 else PB
            Px.wait_r(p, "tensor")
            if qc == 0 and kt == 0:
                p.wait("tensor", s_dve, m_kT_w)
                p.wait("tensor", s_dve, m_uT_w)
                p.wait("tensor", s_act, m_uN_w)
                p.wait("tensor", s_act, m_wo)
                p.wait("vector", s_ldw, 16)
            if kt == 0:
                expT.wait_r(p, "tensor")
            mmx = p.op("tensor", lambda E, kt=kt, qc=qc, Px=Px: E.matmul(
                Px.tile[:, 0:512], kT[:, kt * 128:(kt + 1) * 128], uT[:, qc * 512:(qc + 1) * 512],
                start=True, stop=True), s_pe)
            Px.w = (s_pe, mmx)
            Px.wait_w(p, "scalar")
            if kt == 0:
                expT.wait_r(p, "scalar")
            mex = p.opw("scalar", lambda E, kt=kt, Px=Px: E.activation(
                expT.tile[:, kt * 512:(kt + 1) * 512], Px.tile[:, 0:512], AF.Exp, scale=1.0 / 128.0), s_act)
            Px.r = (s_act, mex)
            m_exp_tiles.append(mex)
        expT.w = (s_act, m_exp_tiles[-1])

        zacc.wait_r(p, "vector")
        p.wait("vector", s_act, m_exp_tiles[1])
        p.op("vector", lambda E: E.tensor_add(zacc.tile[:], expT.tile[:, 0:512], expT.tile[:, 512:1024]))
        mz = None
        for kt in range(2, 32):
            p.wait("vector", s_act, m_exp_tiles[kt])
            mz = p.op("vector", lambda E, kt=kt: E.tensor_add(
                zacc.tile[:], zacc.tile[:], expT.tile[:, kt * 512:(kt + 1) * 512]),
                s_dve if kt == 31 else None)
        zacc.w = (s_dve, mz)
        mzv = ones_colsum(zacc, PS1, stripA)
        if debug:
            p.wait("gpsimd", s_dve, mzv)
            mdz = p.op("gpsimd", lambda E, qc=qc: E.dma_start(dbg["z"].ap()[qc:qc + 1, :], stripA[0:1, 0:512]), s_gp, 16)
        p.wait("scalar", s_dve, mzv)
        if debug:
            p.wait("scalar", s_gp, mdz)
        mlz = p.op("scalar", lambda E: E.activation(stripA[0:1, 0:512], stripA[0:1, 0:512], AF.Ln), s_act)
        p.wait("scalar", s_act, mlz)
        mrz = p.op("scalar", lambda E: E.activation(stripA[0:1, 0:512], stripA[0:1, 0:512], AF.Exp, scale=-1.0), s_act)
        if debug:
            p.wait("gpsimd", s_act, mrz)
            mdrz = p.op("gpsimd", lambda E, qc=qc: E.dma_start(dbg["rz"].ap()[qc:qc + 1, :], stripA[0:1, 0:512]), s_gp, 16)
            p.wait("tensor", s_gp, mdrz)
        mrzb = bcast(stripA, rz_b, PC, s_act, mrz)

        PD.wait_r(p, "tensor")
        p.wait("tensor", s_act, m_exp_tiles[-1])
        def mmout(E):
            for kt in range(32):
                i = E.matmul(PD.tile[:, 0:512], uN[:, kt * 128:(kt + 1) * 128],
                             expT.tile[:, kt * 512:(kt + 1) * 512], start=(kt == 0), stop=(kt == 31))
            return i
        mmo = p.op("tensor", mmout, s_pe)
        PD.w = (s_pe, mmo)
        expT.r = (s_pe, mmo)
        if debug and qc == 0:
            p.wait("gpsimd", s_act, m_exp_tiles[-1])
            mde = p.op("gpsimd", lambda E: E.dma_start(
                dbg["expt"].ap().bitcast(BF16)[:, 0:32 * 512], expT.tile[:]), s_gp, 16)
        PD.wait_w(p, "vector")
        obuf.wait_r(p, "vector")
        p.wait("vector", s_dve, mrzb)
        mob = p.opw("vector", lambda E: E.tensor_mul(obuf.tile[:, 0:512], PD.tile[:, 0:512], rz_b[:]), s_dve)
        PD.r = (s_dve, mob)
        obuf.w = (s_dve, mob)
        if debug:
            p.wait("gpsimd", s_dve, mob)
            mdo = p.op("gpsimd", lambda E, qc=qc: E.dma_start(
                dbg["obuf"].ap()[qc].bitcast(BF16)[:, 0:512], obuf.tile[:, 0:512]), s_gp, 16)
            p.wait("tensor", s_gp, mdo)

        X_b = xst[qc % 2]
        X_b.wait_r(p, "scalar")
        mxl = p.op("scalar", lambda E, qc=qc, X_b=X_b: E.dma_start(
            X_b.tile.rearrange("p (cc t) -> p cc t", cc=NCC), xsub_ap(XT_OWN, qc)), s_ldx, 16)
        X_b.w = (s_ldx, mxl)
        mx1 = None
        for nt in range(16):
            Px = PA if nt % 2 == 0 else PB
            Px.wait_r(p, "tensor")
            obuf.wait_w(p, "tensor")
            mmy = p.op("tensor", lambda E, nt=nt, Px=Px: E.matmul(
                Px.tile[:, 0:512], wo_b[:, nt * 128:(nt + 1) * 128], obuf.tile[:, 0:512],
                start=True, stop=True), s_pe)
            Px.w = (s_pe, mmy)
            if nt == 15:
                obuf.r = (s_pe, mmy)
            Px.wait_w(p, "vector")
            X_b.wait_w(p, "vector")
            mx1 = p.opw("vector", lambda E, Px=Px, nt=nt, X_b=X_b: E.scalar_tensor_tensor(
                X_b.tile[:, nt * 512:(nt + 1) * 512], Px.tile[:, 0:512], bo_t[:, nt:nt + 1],
                X_b.tile[:, nt * 512:(nt + 1) * 512], ALU.add, ALU.add), s_dve)
            Px.r = (s_dve, mx1)
        emit_scan(18)
        p.wait("gpsimd", s_dve, mx1)
        ms = p.op("gpsimd", lambda E, qc=qc, X_b=X_b: E.dma_start(
            xsub_ap(X1_HBM, qc), X_b.tile.rearrange("p (cc t) -> p cc t", cc=NCC)), s_gp, 16)
        X_b.r = (s_gp, ms)
        m_x1_store.append(ms)

    if debug:
        p.wait("gpsimd", s_gp, m_x1_store[-1])
        p.op("gpsimd", lambda E: E.dma_start(dbg["x1"].ap(), X1_HBM.ap()), s_gp, 16)

    for nm in ("expT", "zacc", "obuf", "rz_b", "xst0", "xst1",
               "wk_b", "wv_b", "wo_b", "kT", "uT", "uN"):
        ar.free(nm)

    m_scales = finish_scan()
    for nm in ("scan0", "scan1", "scan_acc", "scan_m"):
        ar.free(nm)
    snap_a1 = p.snap()
    CF = ("vector", "scalar", "tensor", "sync", "gpsimd")
    for _e in CF:
        p.wait_snap(_e, snap_a1)
    x1c = Buf(AL("x1c", NCC * 512, eng_fence=CF), "x1c")
    xqT = Buf(AL("xqT", NCC * 512 // 2, BF16, eng_fence=CF), "xqT")
    hbuf = Buf(AL("hbuf", NOT * 512 // 2, BF16, eng_fence=CF), "hbuf")
    ab_b = AL("ab_b", 512, eng_fence=CF)
    be_b = AL("be_b", 512, eng_fence=CF)
    ga_b = AL("ga_b", 512, eng_fence=CF)
    cb_b = AL("cb_b", 512, eng_fence=CF)
    c3_b = AL("c3_b", 512, eng_fence=CF)
    qint = Buf(AL("qint", 512, I32, eng_fence=CF), "qint")
    sil = Buf(AL("sil", 512, eng_fence=CF), "sil")
    outb = [Buf(AL(f"outb{i}", 512, eng_fence=CF), f"outb{i}") for i in range(2)]
    stripE = AL("stripE", 512, eng_fence=CF)
    stripF = AL("stripF", 512, eng_fence=CF)

    m_tern_gu = {}
    m_tern_d = {}

    for chunk in range(NCHUNK):
        CFc = ("vector", "scalar", "tensor", "sync", "gpsimd")
        sqf = Buf(AL("sqf", 2048, eng_fence=CFc), "sqf")
        trf1 = Buf(AL("trf1", 512, eng_fence=CFc), "trf1")
        trf2 = Buf(AL("trf2", 512, eng_fence=CFc), "trf2")
        mxt = Buf(AL("mxt", 512, eng_fence=CFc), "mxt")
        mnt = Buf(AL("mnt", 512, eng_fence=CFc), "mnt")

        x1c.wait_r(p, "scalar")
        p.wait("scalar", s_gp, m_x1_store[chunk])
        mxc = p.op("scalar", lambda E, chunk=chunk: E.dma_start(
            x1c.tile.rearrange("p (cc t) -> p cc t", cc=NCC), xsub_ap(X1_HBM, chunk)), s_ldx, 16)
        x1c.w = (s_ldx, mxc)

        x1c.wait_w(p, "vector")
        trf1.wait_r(p, "vector")
        def tree_sum2(E):
            E.tensor_add(trf1.tile[:], x1c.tile[:, 0:512], x1c.tile[:, 512:1024])
            for cc in range(2, NCC):
                i = E.tensor_add(trf1.tile[:], trf1.tile[:], x1c.tile[:, cc * 512:(cc + 1) * 512])
            return i
        mt1 = p.op("vector", tree_sum2, s_dve)
        trf1.w = (s_dve, mt1)
        trf2.wait_r(p, "vector")
        mt2 = None
        for qtr in range(4):
            sqf.wait_r(p, "scalar")
            x1c.wait_w(p, "scalar")
            msq = p.op("scalar", lambda E, qtr=qtr: E.square(
                sqf.tile[:], x1c.tile[:, qtr * 2048:(qtr + 1) * 2048]), s_act)
            sqf.w = (s_act, msq)
            p.wait("vector", s_act, msq)
            def tree_sq2(E, qtr=qtr):
                if qtr == 0:
                    E.tensor_add(trf2.tile[:], sqf.tile[:, 0:512], sqf.tile[:, 512:1024])
                    rng_ = range(2, 4)
                else:
                    rng_ = range(0, 4)
                for cc in rng_:
                    i = E.tensor_add(trf2.tile[:], trf2.tile[:], sqf.tile[:, cc * 512:(cc + 1) * 512])
                return i
            mt2 = p.op("vector", tree_sq2, s_dve)
            sqf.r = (s_dve, mt2)
        trf2.w = (s_dve, mt2)
        mxt.wait_r(p, "vector")
        mnt.wait_r(p, "vector")
        def tree_mx(E):
            E.tensor_tensor(mxt.tile[:], x1c.tile[:, 0:512], x1c.tile[:, 512:1024], ALU.max)
            E.tensor_tensor(mnt.tile[:], x1c.tile[:, 0:512], x1c.tile[:, 512:1024], ALU.min)
            for cc in range(2, NCC):
                E.tensor_tensor(mxt.tile[:], mxt.tile[:], x1c.tile[:, cc * 512:(cc + 1) * 512], ALU.max)
                i = E.tensor_tensor(mnt.tile[:], mnt.tile[:], x1c.tile[:, cc * 512:(cc + 1) * 512], ALU.min)
            return i
        mt3 = p.op("vector", tree_mx, s_dve)
        mxt.w = (s_dve, mt3)
        mnt.w = (s_dve, mt3)

        ones_colsum(trf1, PS1, stripA)
        ones_colsum(trf2, PS1, stripB)
        dmx = Buf(AL("dmx", 256, BF16, eng_fence=CFc, align=64), "dmx")
        dmn = Buf(AL("dmn", 256, BF16, eng_fence=CFc, align=64), "dmn")
        ttb = Buf(AL("ttb", 64, BF16, eng_fence=CFc), "ttb")
        mxt.wait_w(p, "scalar")
        mnt.wait_w(p, "scalar")
        dmx.wait_r(p, "scalar")
        dmn.wait_r(p, "scalar")
        p.op("scalar", lambda E: E.copy(dmx.tile[:], mxt.tile[:]))
        mcst = p.op("scalar", lambda E: E.copy(dmn.tile[:], mnt.tile[:]), s_act)
        dmx.w = (s_act, mcst)
        dmn.w = (s_act, mcst)
        mxt.r = (s_act, mcst)
        mnt.r = (s_act, mcst)
        mrd = None
        for di, acc in enumerate((dmx, dmn)):
            for i4 in range(4):
                ttb.wait_r(p, "scalar")
                acc.wait_w(p, "scalar")
                mtr = p.op("scalar", lambda E, acc=acc, i4=i4: E.dma_start(
                    ttb.tile[:], acc.tile[:, i4 * 128:(i4 + 1) * 128], transpose=True), s_ldx, 16)
                ttb.w = (s_ldx, mtr)
                ttb.wait_w(p, "vector")
                mrd = p.op("vector", lambda E, di=di, i4=i4: E.tensor_reduce(
                    tokv[:, di * 4 + i4: di * 4 + i4 + 1], ttb.tile[:], AX.X,
                    ALU.max if di == 0 else ALU.min), s_dve)
                ttb.r = (s_dve, mrd)
            acc.r = (s_ldx, mtr)
        bounce_strips(0, 4, stripC[0:1, 0:512], s_dve, mrd)
        mst_l = bounce_strips(4, 4, stripD[0:1, 0:512], s_dve, mrd)
        p.wait("vector", mst_l[0], mst_l[1])
        sA = stripA[0:1, 0:512]; sB = stripB[0:1, 0:512]
        sC = stripC[0:1, 0:512]; sD = stripD[0:1, 0:512]
        sE = stripE[0:1, 0:512]; sF = stripF[0:1, 0:512]
        p.vop(lambda E: E.tensor_scalar(sA, sA, 1.0 / D, None, ALU.mult), s_dve)
        p.vop(lambda E: E.tensor_scalar(sB, sB, 1.0 / D, EPS, ALU.mult, ALU.add), s_dve)
        p.vop(lambda E: E.scalar_tensor_tensor(sE, sA, -1.0, sA, ALU.mult, ALU.mult), s_dve)
        mvar = p.vop(lambda E: E.tensor_add(sB, sB, sE), s_dve)
        p.wait("scalar", s_dve, mvar)
        mln = p.op("scalar", lambda E: E.activation(sB, sB, AF.Ln), s_act)
        p.wait("scalar", s_act, mln)
        mr2 = p.op("scalar", lambda E: E.activation(sB, sB, AF.Exp, scale=-0.5), s_act)
        p.vop(lambda E: E.tensor_sub(sE, sC, sA), s_dve)
        p.vop(lambda E: E.tensor_sub(sF, sA, sD), s_dve)
        p.vop(lambda E: E.tensor_tensor(sE, sE, sF, ALU.max), s_dve)
        p.wait("vector", s_act, mr2)
        p.vop(lambda E: E.tensor_mul(sE, sE, sB), s_dve)
        mxa = p.vop(lambda E: E.tensor_scalar(sE, sE, EPS, None, ALU.max), s_dve)
        p.wait("scalar", s_dve, mxa)
        mln2 = p.op("scalar", lambda E: E.activation(sF, sE, AF.Ln), s_act)
        p.wait("scalar", s_act, mln2)
        mrcp = p.op("scalar", lambda E: E.activation(sF, sF, AF.Exp, scale=-1.0), s_act)
        p.wait("vector", s_act, mrcp)
        p.vop(lambda E: E.tensor_mul(sF, sB, sF), s_dve)
        ma_ = p.vop(lambda E: E.tensor_scalar(sF, sF, 127.0, None, ALU.mult), s_dve)
        mb_ = p.vop(lambda E: E.scalar_tensor_tensor(sD, sA, -1.0, sF, ALU.mult, ALU.mult), s_dve)
        mgm = p.vop(lambda E: E.tensor_scalar(sE, sE, 1.0 / 127.0, None, ALU.mult), s_dve)
        bcast(stripF, ab_b, PC, s_dve, ma_)
        bcast(stripD, be_b, PD, s_dve, mb_)
        mgb = bcast(stripE, ga_b, PE_, s_dve, mgm)

        mq = None
        for cc in range(NCC):
            qint.wait_r(p, "vector")
            xqT.wait_r(p, "vector")
            sil.wait_r(p, "vector")
            sil.wait_w(p, "vector")
            p.wait("vector", s_dve, mgb)
            p.op("vector", lambda E, cc=cc: E.tensor_mul(sil.tile[:], x1c.tile[:, cc * 512:(cc + 1) * 512], ab_b[:]))
            mqi = p.op("vector", lambda E: E.scalar_tensor_tensor(
                qint.tile[:], sil.tile[:], 512.0, be_b[:], ALU.add, ALU.add), s_dve)
            qint.w = (s_dve, mqi)
            mq = p.opw("vector", lambda E, cc=cc: E.scalar_tensor_tensor(
                xqT.tile[:, cc * 512:(cc + 1) * 512], qint.tile[:], -512.0, ga_b[:],
                ALU.add, ALU.mult), s_dve)
            qint.r = (s_dve, mq)
            sil.w = (s_dve, mq)
            sil.r = (s_dve, mq)
        xqT.w = (s_dve, mq)
        if debug and chunk == 0:
            p.wait("gpsimd", s_dve, mq)
            p.op("gpsimd", lambda E: E.dma_start(
                dbg["xq"].ap().bitcast(BF16)[:, 0:NCC * 512], xqT.tile[:]), s_gp, 16)

        for nm in ("sqf", "trf1", "trf2", "mxt", "mnt", "dmx", "dmn"):
            ar.free(nm)
        if chunk == 0:
            wgu = [Buf(AL(f"wgu{i}", 2 * D, eng_fence=CFc), f"wgu{i}") for i in range(2)]
            tpq0 = Buf(AL("tpq0", D // 2, BF16, eng_fence=CFc), "tpq0")
            tpq1 = Buf(AL("tpq1", D // 2, BF16, eng_fence=CFc), "tpq1")
        tg = [Buf(AL(f"tg{i}", D // 2, BF16, eng_fence=CFc), f"tg{i}") for i in range(2)]
        tu = [Buf(AL(f"tu{i}", D // 2, BF16, eng_fence=CFc), f"tu{i}") for i in range(2)]
        m_h = None
        for ot in range(NOT):
            par = ot % 2
            Tg, Tu = tg[par], tu[par]
            if chunk == 0:
                Wb = wgu[par]
                Wb.wait_r(p, "sync")
                p.op("sync", lambda E, ot=ot, Wb=Wb: E.dma_start(
                    Wb.tile[:, 0:D], WGU_T.ap()[ot, 0]), s_ldw, 16)
                mw = p.op("sync", lambda E, ot=ot, Wb=Wb: E.dma_start(
                    Wb.tile[:, D:2 * D], WGU_T.ap()[ot, 1]), s_ldw, 16)
                Wb.w = (s_ldw, mw)
                Wb.wait_w(p, "vector")
                for half, Tx, sc_p, sc_n in ((0, Tg, 0, 3), (1, Tu, 1, 4)):
                    Tx.wait_r(p, "vector")
                    tpq0.wait_r(p, "vector")
                    tpq1.wait_r(p, "vector")
                    p.wait("vector", s_dve, m_scales)
                    p.op("vector", lambda E, Wb=Wb, half=half, sc_p=sc_p: E.tensor_scalar(
                        tpq0.tile[:], Wb.tile[:, half * D:(half + 1) * D], scl[:, sc_p:sc_p + 1],
                        0.5, ALU.mult, ALU.is_ge))
                    mpq = p.op("vector", lambda E, Wb=Wb, half=half, sc_n=sc_n: E.tensor_scalar(
                        tpq1.tile[:], Wb.tile[:, half * D:(half + 1) * D], scl[:, sc_n:sc_n + 1],
                        0.5, ALU.mult, ALU.is_ge), s_dve)
                    tpq0.w = (s_dve, mpq)
                    tpq1.w = (s_dve, mpq)
                    if half == 1:
                        Wb.r = (s_dve, mpq)
                    mtn = p.opw("vector", lambda E, Tx=Tx: E.tensor_tensor(
                        Tx.tile[:], tpq0.tile[:], tpq1.tile[:], ALU.subtract), s_dve)
                    Tx.w = (s_dve, mtn)
                    tpq0.r = (s_dve, mtn)
                    tpq1.r = (s_dve, mtn)
                Tg.wait_w(p, "gpsimd")
                Tu.wait_w(p, "gpsimd")
                mst1 = p.op("gpsimd", lambda E, ot=ot, Tg=Tg: E.dma_start(
                    TGU_HBM.ap()[ot, 0], Tg.tile[:]), s_gp, 16)
                mst2 = p.op("gpsimd", lambda E, ot=ot, Tu=Tu: E.dma_start(
                    TGU_HBM.ap()[ot, 1], Tu.tile[:]), s_gp, 16)
                m_tern_gu[ot] = mst2
                if debug and chunk == 0 and ot == 0:
                    p.wait("gpsimd", s_ldw, Wb.w[1])
                    p.wait("gpsimd", s_dve, s_dve.n)
                    p.op("gpsimd", lambda E, Wb=Wb: E.dma_start(dbg["wgu0"].ap(), Wb.tile[:]), s_gp, 16)
                    p.op("gpsimd", lambda E: E.dma_start(
                        dbg["tp0"].ap().bitcast(BF16)[:, 0:D], tpq0.tile[:]), s_gp, 16)
                    p.op("gpsimd", lambda E: E.dma_start(
                        dbg["tq0"].ap().bitcast(BF16)[:, 0:D], tpq1.tile[:]), s_gp, 16)
                    p.op("gpsimd", lambda E, Tg=Tg: E.dma_start(
                        dbg["tg0"].ap().bitcast(BF16)[:, 0:D], Tg.tile[:]), s_gp, 16)
                    m_dbg0 = p.op("gpsimd", lambda E, Tu=Tu: E.dma_start(
                        dbg["tu0"].ap().bitcast(BF16)[:, 0:D], Tu.tile[:]), s_gp, 16)
                    p.wait("gpsimd", s_pe, s_pe.n)
                    m_dbg0 = p.op("gpsimd", lambda E, Wb=Wb: E.dma_start(dbg["wgu0c"].ap(), Wb.tile[:]), s_gp, 16)
                    Wb.r = (s_gp, m_dbg0)
                    tpq0.r = (s_gp, m_dbg0)
                    tpq1.r = (s_gp, m_dbg0)
                    p.wait("tensor", s_gp, m_dbg0)
            else:
                Tg.wait_r(p, "sync")
                Tu.wait_r(p, "sync")
                p.wait("sync", s_gp, m_tern_gu[ot])
                mw1 = p.op("sync", lambda E, ot=ot, Tg=Tg: E.dma_start(
                    Tg.tile[:], TGU_HBM.ap()[ot, 0]), s_ldw, 16)
                mw2 = p.op("sync", lambda E, ot=ot, Tu=Tu: E.dma_start(
                    Tu.tile[:], TGU_HBM.ap()[ot, 1]), s_ldw, 16)
                Tg.w = (s_ldw, mw1)
                Tu.w = (s_ldw, mw2)

            Pg = PA if par == 0 else PC
            Pu = PB if par == 0 else PD
            for (Px, Tx) in ((Pg, Tg), (Pu, Tu)):
                Px.wait_r(p, "tensor")
                Tx.wait_w(p, "tensor")
                if ot == 0:
                    xqT.wait_w(p, "tensor")
                def mmf(E, Px=Px, Tx=Tx):
                    for cc in range(NCC):
                        i = E.matmul(Px.tile[:, 0:512], Tx.tile[:, cc * 128:(cc + 1) * 128],
                                     xqT.tile[:, cc * 512:(cc + 1) * 512],
                                     start=(cc == 0), stop=(cc == NCC - 1))
                    return i
                mmx = p.op("tensor", mmf, s_pe)
                Px.w = (s_pe, mmx)
                Tx.r = (s_pe, mmx)
                if ot == NOT - 1 and Px is Pu:
                    xqT.r = (s_pe, mmx)
            Pg.wait_w(p, "scalar")
            sil.wait_r(p, "scalar")
            msl = p.opw("scalar", lambda E, Pg=Pg: E.activation(
                sil.tile[:], Pg.tile[:, 0:512], AF.Silu, scale=scl[:, 6:7]), s_act)
            Pg.r = (s_act, msl)
            sil.w = (s_act, msl)
            Pu.wait_w(p, "vector")
            sil.wait_w(p, "vector")
            hbuf.wait_r(p, "vector")
            m_h = p.opw("vector", lambda E, Pu=Pu, ot=ot: E.tensor_mul(
                hbuf.tile[:, ot * 512:(ot + 1) * 512], sil.tile[:], Pu.tile[:, 0:512]), s_dve)
            Pu.r = (s_dve, m_h)
            sil.r = (s_dve, m_h)
        hbuf.w = (s_dve, m_h)
        if debug and chunk == 0:
            p.wait("gpsimd", s_dve, m_h)
            mdh = p.op("gpsimd", lambda E: E.dma_start(
                dbg["h"].ap().bitcast(BF16)[:, 0:NOT * 512], hbuf.tile[:]), s_gp, 16)
            p.wait("vector", s_gp, mdh)

        mxh = Buf(AL("mxh", 256, BF16, eng_fence=CFc, align=64), "mxh")
        mnh = Buf(AL("mnh", 256, BF16, eng_fence=CFc, align=64), "mnh")
        mxh.wait_r(p, "vector")
        mnh.wait_r(p, "vector")
        def htree(E):
            E.tensor_tensor(mxh.tile[:], hbuf.tile[:, 0:512], hbuf.tile[:, 512:1024], ALU.max)
            E.tensor_tensor(mnh.tile[:], hbuf.tile[:, 0:512], hbuf.tile[:, 512:1024], ALU.min)
            for ot in range(2, NOT):
                E.tensor_tensor(mxh.tile[:], mxh.tile[:], hbuf.tile[:, ot * 512:(ot + 1) * 512], ALU.max)
                i = E.tensor_tensor(mnh.tile[:], mnh.tile[:], hbuf.tile[:, ot * 512:(ot + 1) * 512], ALU.min)
            return i
        mht = p.op("vector", htree, s_dve)
        mxh.w = (s_dve, mht)
        mnh.w = (s_dve, mht)
        mrd2 = None
        for di, acc in enumerate((mxh, mnh)):
            for i4 in range(4):
                ttb.wait_r(p, "scalar")
                acc.wait_w(p, "scalar")
                mtr = p.op("scalar", lambda E, acc=acc, i4=i4: E.dma_start(
                    ttb.tile[:], acc.tile[:, i4 * 128:(i4 + 1) * 128], transpose=True), s_ldx, 16)
                ttb.w = (s_ldx, mtr)
                ttb.wait_w(p, "vector")
                mrd2 = p.op("vector", lambda E, di=di, i4=i4: E.tensor_reduce(
                    tokv[:, 8 + di * 4 + i4: 8 + di * 4 + i4 + 1], ttb.tile[:], AX.X,
                    ALU.max if di == 0 else ALU.min), s_dve)
                ttb.r = (s_dve, mrd2)
            acc.r = (s_ldx, mtr)
        p.wait("vector", s_dve, mrd2)
        p.vop(lambda E: E.tensor_scalar(tokv[:, 12:16], tokv[:, 12:16], -1.0, None, ALU.mult), s_dve)
        mhb = p.vop(lambda E: E.tensor_tensor(tokv[:, 8:12], tokv[:, 8:12], tokv[:, 12:16], ALU.max), s_dve)
        msh_l = bounce_strips(8, 4, stripC[0:1, 0:512], s_dve, mhb)
        p.wait("vector", msh_l[0], msh_l[1])
        if debug and chunk == 0:
            p.wait("gpsimd", s_dve, s_dve.n)
            p.wait("gpsimd", s_ldx, s_ldx.n)
            p.op("gpsimd", lambda E: E.dma_start(dbg["tokv"].ap(), tokv[:]), s_gp, 16)
            p.op("gpsimd", lambda E: E.dma_start(dbg["mxh"].ap().bitcast(BF16)[:, 0:512], mxh.tile[:]), s_gp, 16)
            mdt = p.op("gpsimd", lambda E: E.dma_start(dbg["habs"].ap(), stripC[0:1, 0:512]), s_gp, 16)
            p.wait("vector", s_gp, mdt)
        mcl = p.vop(lambda E: E.tensor_scalar(sC, sC, scl[0:1, 9:10], EPS, ALU.mult, ALU.max), s_dve)
        mc3 = p.vop(lambda E: E.tensor_scalar(sD, sC, scl[0:1, 8:9], None, ALU.mult), s_dve)
        p.wait("scalar", s_dve, mcl)
        mlh = p.op("scalar", lambda E: E.activation(sE, sC, AF.Ln), s_act)
        p.wait("scalar", s_act, mlh)
        mrc = p.op("scalar", lambda E: E.activation(sE, sE, AF.Exp, scale=-1.0), s_act)
        p.wait("vector", s_act, mrc)
        mcb = p.vop(lambda E: E.tensor_scalar(sE, sE, scl[0:1, 7:8], None, ALU.mult), s_dve)
        mcbb = bcast(stripE, cb_b, PC, s_dve, mcb)
        mc3b = bcast(stripD, c3_b, PD, s_dve, mc3)
        if debug and chunk == 0:
            p.wait("gpsimd", s_dve, mc3b)
            mdc = p.op("gpsimd", lambda E: E.dma_start(dbg["cbb"].ap(), cb_b[:]), s_gp, 16)
            mdc = p.op("gpsimd", lambda E: E.dma_start(dbg["c3b"].ap(), c3_b[:]), s_gp, 16)
            p.wait("vector", s_gp, mdc)

        mhq = None
        for ot in range(NOT):
            sil.wait_r(p, "vector")
            sil.wait_w(p, "vector")
            p.wait("vector", s_dve, mc3b)
            p.op("vector", lambda E, ot=ot: E.tensor_mul(sil.tile[:], hbuf.tile[:, ot * 512:(ot + 1) * 512], cb_b[:]))
            qint.wait_r(p, "vector")
            mqi = p.op("vector", lambda E: E.tensor_scalar(qint.tile[:], sil.tile[:], 512.0, None, ALU.add), s_dve)
            qint.w = (s_dve, mqi)
            mhq = p.opw("vector", lambda E, ot=ot: E.tensor_scalar(
                hbuf.tile[:, ot * 512:(ot + 1) * 512], qint.tile[:], -512.0, None, ALU.add), s_dve)
            qint.r = (s_dve, mhq)
            sil.w = (s_dve, mhq)
            sil.r = (s_dve, mhq)
        hbuf.w = (s_dve, mhq)
        if debug and chunk == 0:
            p.wait("gpsimd", s_dve, mhq)
            p.op("gpsimd", lambda E: E.dma_start(
                dbg["hq"].ap().bitcast(BF16)[:, 0:NOT * 512], hbuf.tile[:]), s_gp, 16)

        for nm in ("mxh", "mnh", "tg0", "tg1", "tu0", "tu1"):
            ar.free(nm)
        if chunk == 0:
            ar.free("wgu0"); ar.free("wgu1")
        CF2 = ("vector", "scalar", "tensor", "sync", "gpsimd")
        if chunk == 0:
            wd_st = Buf(AL("wd_st", I_PAD, eng_fence=CF2), "wd_st")
        td = [Buf(AL(f"td{i}", I_PAD // 2, BF16, eng_fence=CF2), f"td{i}") for i in range(2)]

        mo = None
        for nt in range(16):
            par = nt % 2
            Td = td[par]
            if chunk == 0:
                wd_st.wait_r(p, "sync")
                mw = p.op("sync", lambda E, nt=nt: E.dma_start(wd_st.tile[:], WD_T.ap()[nt]), s_ldw, 16)
                wd_st.w = (s_ldw, mw)
                Td.wait_r(p, "vector")
                tpq0.wait_r(p, "vector")
                tpq1.wait_r(p, "vector")
                wd_st.wait_w(p, "vector")
                mtd = None
                for hh in range(4):
                    lo, hi = hh * (I_PAD // 4), (hh + 1) * (I_PAD // 4)
                    p.op("vector", lambda E, lo=lo, hi=hi: E.tensor_scalar(
                        tpq0.tile[:, 0:hi - lo], wd_st.tile[:, lo:hi], scl[:, 2:3], 0.5, ALU.mult, ALU.is_ge))
                    mpq = p.op("vector", lambda E, lo=lo, hi=hi: E.tensor_scalar(
                        tpq1.tile[:, 0:hi - lo], wd_st.tile[:, lo:hi], scl[:, 5:6], 0.5, ALU.mult, ALU.is_ge), s_dve)
                    mtd = p.opw("vector", lambda E, Td=Td, lo=lo, hi=hi: E.tensor_tensor(
                        Td.tile[:, lo:hi], tpq0.tile[:, 0:hi - lo], tpq1.tile[:, 0:hi - lo], ALU.subtract), s_dve)
                tpq0.r = (s_dve, mtd)
                tpq1.r = (s_dve, mtd)
                wd_st.r = (s_dve, mpq)
                Td.w = (s_dve, mtd)
                Td.wait_w(p, "gpsimd")
                mstd = p.op("gpsimd", lambda E, nt=nt, Td=Td: E.dma_start(
                    TD_HBM.ap()[nt], Td.tile[:]), s_gp, 16)
                m_tern_d[nt] = mstd
            else:
                Td.wait_r(p, "sync")
                p.wait("sync", s_gp, m_tern_d[nt])
                mw = p.op("sync", lambda E, nt=nt, Td=Td: E.dma_start(Td.tile[:], TD_HBM.ap()[nt]), s_ldw, 16)
                Td.w = (s_ldw, mw)

            Px = PA if par == 0 else PB
            Px.wait_r(p, "tensor")
            Td.wait_w(p, "tensor")
            if nt == 0:
                hbuf.wait_w(p, "tensor")
            def mmd(E, Px=Px, Td=Td):
                for ic in range(NOT):
                    i = E.matmul(Px.tile[:, 0:512], Td.tile[:, ic * 128:(ic + 1) * 128],
                                 hbuf.tile[:, ic * 512:(ic + 1) * 512],
                                 start=(ic == 0), stop=(ic == NOT - 1))
                return i
            mmx = p.op("tensor", mmd, s_pe)
            Px.w = (s_pe, mmx)
            Td.r = (s_pe, mmx)
            if nt == 15:
                hbuf.r = (s_pe, mmx)
            Ob = outb[par]
            Px.wait_w(p, "vector")
            Ob.wait_r(p, "vector")
            p.wait("vector", s_dve, mc3b)
            p.op("vector", lambda E, Px=Px, Ob=Ob: E.tensor_mul(Ob.tile[:], Px.tile[:, 0:512], c3_b[:]))
            mo = p.opw("vector", lambda E, Ob=Ob, nt=nt: E.tensor_add(
                Ob.tile[:], Ob.tile[:], x1c.tile[:, nt * 512:(nt + 1) * 512]), s_dve)
            Px.r = (s_dve, mo)
            Ob.w = (s_dve, mo)
            if nt == 15:
                x1c.r = (s_dve, mo)
            Ob.wait_w(p, "gpsimd")
            mso = p.op("gpsimd", lambda E, nt=nt, chunk=chunk, Ob=Ob: E.dma_start(
                bass.AP(OUT_T, nt * 128 * SOWN + chunk * 512, [[SOWN, 128], [1, 512]]),
                Ob.tile[:]), s_gp, 16)
            Ob.r = (s_gp, mso)
        if chunk == 0:
            ar.free("wd_st"); ar.free("tpq0"); ar.free("tpq1")
        ar.free("td0"); ar.free("td1"); ar.free("ttb")

    if debug:
        p.wait("gpsimd", s_ldw, s_ldw.n)
        p.wait("gpsimd", s_dve, s_dve.n)
        p.op("gpsimd", lambda E: E.dma_start(dbg["wgu0b"].ap(), ar.base[:, 33416:33416 + 4096]), s_gp, 16)
    p.wait("gpsimd", s_gp, s_gp.n)

    blk = nc.Block()
    blk.__enter__()
    p.emit(blk)
    blk.__exit__(None, None, None)
    for g in reversed(guards):
        g.__exit__(None, None, None)
    ar._g.__exit__(None, None, None)
    return nc


def host_prepare(inputs):
    f32 = np.float32
    x = np.ascontiguousarray(inputs["L0_attn_q"], f32).reshape(16384, D)
    xt = np.ascontiguousarray(x.T)
    wk = np.ascontiguousarray(np.asarray(inputs["L0_attn_Wk"], f32).reshape(NCC, 128, 128))
    wv = np.ascontiguousarray(np.asarray(inputs["L0_attn_Wv"], f32).reshape(NCC, 128, 128))
    bk = np.ascontiguousarray(np.asarray(inputs["L0_attn_bk"], f32).reshape(128, 1))
    bv = np.ascontiguousarray(np.asarray(inputs["L0_attn_bv"], f32).reshape(128, 1))
    wo = np.ascontiguousarray(np.asarray(inputs["L0_attn_Wo"], f32).reshape(16, 128, D))
    bo_t = np.ascontiguousarray(np.asarray(inputs["L0_attn_bo"], f32).reshape(16, 128).T)
    wg = np.asarray(inputs["L0_ffn_w_gate"], f32)
    wu = np.asarray(inputs["L0_ffn_w_up"], f32)
    wd = np.asarray(inputs["L0_ffn_w_down"], f32)
    wgp = np.zeros((I_PAD, D), f32); wgp[:I_RAW] = wg
    wup = np.zeros((I_PAD, D), f32); wup[:I_RAW] = wu
    wdp = np.zeros((D, I_PAD), f32); wdp[:, :I_RAW] = wd
    wg_t = np.transpose(wgp.reshape(NOT, 128, NCC, 128), (0, 3, 2, 1)).reshape(NOT, 128, D)
    wu_t = np.transpose(wup.reshape(NOT, 128, NCC, 128), (0, 3, 2, 1)).reshape(NOT, 128, D)
    wgu_t = np.ascontiguousarray(np.stack([wg_t, wu_t], axis=1))
    wd_t = np.ascontiguousarray(
        np.transpose(wdp.reshape(16, 128, NOT, 128), (0, 3, 2, 1)).reshape(16, 128, I_PAD))
    shared = dict(wk=wk, wv=wv, bk=bk, bv=bv, wo=wo, bo_t=bo_t, wgu_t=wgu_t, wd_t=wd_t)
    in_maps = []
    for c in range(8):
        nb = c ^ 1
        m = dict(shared)
        m["xt_own"] = np.ascontiguousarray(xt[:, c * SOWN:(c + 1) * SOWN])
        m["xt_nb"] = np.ascontiguousarray(xt[:, nb * SOWN:(nb + 1) * SOWN])
        in_maps.append(m)
    return in_maps


def host_finish(results):
    outs = [np.asarray(r["out_t"]) for r in results]
    full = np.concatenate(outs, axis=1)
    return np.ascontiguousarray(full.T).reshape(4, 4096, D)


from concourse.bass_utils import run_bass_kernel_spmd


def kernel(**inputs):
    nc = build(debug=0)
    in_maps = host_prepare(inputs)
    res = run_bass_kernel_spmd(nc, in_maps, core_ids=list(range(8)))
    return host_finish(res.results)
